# version 04
# baseline (speedup 1.0000x reference)
import numpy as np
import ml_dtypes
import concourse.bass as bass
import concourse.mybir as mybir
from concourse.bass_utils import run_bass_kernel_spmd

F32 = mybir.dt.float32
BF16 = mybir.dt.bfloat16
I32 = mybir.dt.int32
AF = mybir.ActivationFunctionType
ALU = mybir.AluOpType
AX = mybir.AxisListType

NCH = 16
T = 512
NI = {0: 64, 1: 44}
HALVES = {0: [(0, 32), (32, 32)], 1: [(0, 24), (24, 20)]}
ATT_SCALE = float(1.0 / np.sqrt(128.0))
EPS = 1e-5

C_SG = {0: 0, 1: 144}
C_SU = {0: 64, 1: 188}
C_SD = {0: 128, 1: 232}
C_QB1 = 248
C_OB1 = 272
C_NCOLS = 288

SUBLAYERS = [("ffn", 0), ("att0", 0), ("ffn", 0), ("ffn", 1), ("att1", 1), ("ffn", 1)]
SEGMENTS = [[0, 1, 2, 3, 4, 5]]

SEM_NAMES = ["pe", "act", "dve", "pool", "sp", "w0", "w1", "io0", "io1", "cst", "st"]
QUEUES = ["pe", "act", "dve", "pool", "sp"]


class Sched:

    def __init__(self):
        self.tasks = {q: [] for q in QUEUES}
        self.cnt = {s: 0 for s in SEM_NAMES}
        self.floor = {}
        self.res = {}

    def emit(self, q, fns, reads=(), writes=(), sem=None, dma=False):
        if not isinstance(fns, (list, tuple)):
            fns = [fns]
        sem = sem or q
        step = 16 if dma else 1
        need = {}

        def add(w):
            if w is not None:
                need[w[0]] = max(need.get(w[0], 0), w[1])

        for k in reads:
            e = self.res.get(k)
            if e is not None:
                add(e[0])
        for k in writes:
            e = self.res.get(k)
            if e is not None:
                add(e[0])
                for r in e[1]:
                    add(r)
        waits = []
        for ps in sorted(need):
            c = need[ps]
            key = (q, ps)
            if self.floor.get(key, 0) >= c:
                continue
            waits.append((ps, c))
            self.floor[key] = c
        self.cnt[sem] += step
        mc = (sem, self.cnt[sem])
        self.tasks[q].append((waits, fns, sem, step))
        for k in reads:
            self.res.setdefault(k, [None, []])[1].append(mc)
        for k in writes:
            self.res[k] = [mc, []]

    def wait_only(self, q, sems_counts):
        self.tasks[q].append((list(sems_counts), [], None, 0))


def ht_keys(c0, c1):
    return [f"HT{k}" for k in range(c0 // 4096, (c1 - 1) // 4096 + 1)]


def build_nc(seg):
    kinds = {SUBLAYERS[i][0] for i in seg}
    layers = {SUBLAYERS[i][1] for i in seg if SUBLAYERS[i][0] == "ffn"}

    nc = bass.Bass("TRN2", target_bir_lowering=False)
    x_in = nc.dram_tensor("x", [1024, 2048], F32, kind="ExternalInput").ap()
    y_out = nc.dram_tensor("y", [1024, 2048], F32, kind="ExternalOutput").ap()
    cs_in = nc.dram_tensor("cs", [128, C_NCOLS], F32, kind="ExternalInput").ap()
    cp_in = nc.dram_tensor("cp", [1, 256], F32, kind="ExternalInput").ap()
    idb_in = nc.dram_tensor("idb", [128, 128], BF16, kind="ExternalInput").ap()
    idf_in = nc.dram_tensor("idf", [128, 128], F32, kind="ExternalInput").ap()
    names = ["cs", "cp", "idb", "idf"]
    dr = {}
    for L in layers:
        dr[f"gu{L}"] = nc.dram_tensor(f"gu{L}", [NI[L], 2, NCH, 128, 128], I32, kind="ExternalInput").ap()
        dr[f"dn{L}"] = nc.dram_tensor(f"dn{L}", [4, NI[L], 128, 512], I32, kind="ExternalInput").ap()
        names += [f"gu{L}", f"dn{L}"]
    if "att0" in kinds:
        dr["w3"] = nc.dram_tensor("w3", [3, NCH, NCH, 128, 128], F32, kind="ExternalInput").ap()
        names += ["w3"]
    if "att1" in kinds:
        dr["qkv1"] = nc.dram_tensor("qkv1", [24, NCH, 128, 128], F32, kind="ExternalInput").ap()
        dr["ow1"] = nc.dram_tensor("ow1", [NCH, NCH, 128, 128], F32, kind="ExternalInput").ap()
        names += ["qkv1", "ow1"]

    S = Sched()

    from contextlib import ExitStack
    with ExitStack() as st_:
        def sb(name, shape, dt):
            return st_.enter_context(nc.sbuf_tensor(name, shape, dt))

        X = [sb("X0", [128, NCH * T], F32), sb("X1", [128, NCH * T], F32)]
        XH = [sb("XHa", [128, NCH * T], BF16), sb("XHb", [128, NCH * T], BF16)]
        HT = sb("HT", [128, 64 * T], BF16)
        WB = [sb("WB0", [128, 4096], BF16), sb("WB1", [128, 4096], BF16)]
        SG = [sb("SGa", [128, T], BF16), sb("SGb", [128, T], BF16)]
        VFS = SG
        T1 = [sb(f"T1{i}", [128, T], F32) for i in range(4)]
        SQ = [sb(f"SQ{i}", [128, T], BF16) for i in range(4)]
        CS = sb("CS", [128, C_NCOLS], F32)
        CP = sb("CP", [1, 256], F32)
        IDB = sb("IDB", [128, 128], BF16)
        IDF = sb("IDF", [128, 128], F32)
        ONESB = sb("ONESB", [128, 1], BF16)
        EPSC = sb("EPSC", [128, 1], F32)
        HS = sb("HS", [128, 32], F32)
        SXR = sb("SXR", [1, 1024], F32)
        ROWA = sb("ROWA", [1, T], F32)
        ROWB = sb("ROWB", [1, T], F32)
        ROWC = sb("ROWC", [1, T], F32)
        SXRv = [SXR, SXR]
        ROWAv = [ROWA, ROWA]
        ROWBv = [ROWB, ROWB]
        ROWCv = [ROWC, ROWC]
        NM = sb("NM", [128, 1], F32)
        BSALL = sb("BSALL", [128, 256], F32)
        LAALL = sb("LAALL", [128, 256], F32)
        DALL = sb("DALL", [128, 256], F32)
        FALL = DALL
        PALL = DALL
        L_ALL = sb("L_ALL", [128, 64], F32)
        B_ALL = sb("B_ALL", [128, 64], F32)
        LNL = sb("LNL", [128, 64], F32)
        BIASV = sb("BIASV", [128, 64], F32)
        PB = st_.enter_context(nc.psum_tensor("PB", [128, 4096], F32))
        sems = {n: st_.enter_context(nc.semaphore(n)) for n in SEM_NAMES}
        block = st_.enter_context(nc.Block())

        def pb(b, n=1):
            return PB[:, b * 512:(b + n) * 512]

        def pbk(b, n=1):
            return [f"pb{i}" for i in range(b, b + n)]

        HTF = HT[:, :].bitcast(F32)
        QT = HT[:, 0:8192]
        KT = HT[:, 8192:16384]
        VT = HT[:, 16384:24576]
        VCH = HT[:, 24576:26624]
        VCH_KEYS = ht_keys(24576, 26624)
        SC = [HT[:, 26624:28672], HT[:, 28672:30720]]
        SCK = [["HT6"], ["HT7"]]
        SCRAP = HT[:, 30720:31232]

        S.emit("sp", lambda e: e.dma_start(out=CS[:, :], in_=cs_in), writes=["CS"], sem="cst", dma=True)
        S.emit("sp", lambda e: e.dma_start(out=CP[:, :], in_=cp_in), writes=["CP"], sem="cst", dma=True)
        S.emit("sp", lambda e: e.dma_start(out=IDB[:, :], in_=idb_in), writes=["IDB"], sem="cst", dma=True)
        S.emit("sp", lambda e: e.dma_start(out=IDF[:, :], in_=idf_in), writes=["IDF"], sem="cst", dma=True)
        S.emit("dve", lambda e: e.memset(ONESB[:, :], 1.0), writes=["ONESB"])
        S.emit("dve", lambda e: e.memset(EPSC[:, :], EPS), writes=["EPSC"])
        for L in layers:
            c0, h0 = C_SD[L], 16 * L
            S.emit("dve", (lambda c0=c0, h0=h0: lambda e: e.tensor_scalar_mul(
                HS[:, h0:h0 + 16], CS[:, c0:c0 + 16], 0.5))(),
                reads=["CS"], writes=["HS"])

        def inload(s):
            for g in range(4):
                slot = g % 2
                TM = HTF[:, slot * 2048:(slot + 1) * 2048]
                tk = f"HT{slot}"
                r0 = s * 512 + g * 128
                S.emit("sp", (lambda TM=TM, r0=r0: lambda e: e.dma_start(
                    out=TM, in_=x_in[r0:r0 + 128, :]))(),
                    writes=[tk], sem=f"io{slot}", dma=True)
                for j in range(NCH):
                    bk = j % 2
                    S.emit("pe", (lambda TM=TM, j=j, bk=bk: lambda e: e.transpose(
                        PB[:, bk * 512:bk * 512 + 128], TM[:, j * 128:(j + 1) * 128], IDF[:, :]))(),
                        reads=[tk, "IDF"], writes=[f"pb{bk}"])
                    S.emit("dve", (lambda s=s, j=j, g=g, bk=bk: lambda e: e.tensor_copy(
                        X[s][:, j * 512 + g * 128:j * 512 + g * 128 + 128],
                        PB[:, bk * 512:bk * 512 + 128]))(),
                        reads=[f"pb{bk}"], writes=[f"X{s}"])

        def outstore(s):
            for g in range(4):
                slot = g % 2
                TM = HTF[:, slot * 2048:(slot + 1) * 2048]
                tk = f"HT{slot}"
                r0 = s * 512 + g * 128
                for j in range(NCH):
                    bk = j % 2
                    S.emit("pe", (lambda s=s, j=j, g=g, bk=bk: lambda e: e.transpose(
                        PB[:, bk * 512:bk * 512 + 128],
                        X[s][:, j * 512 + g * 128:j * 512 + g * 128 + 128], IDF[:, :]))(),
                        reads=[f"X{s}", "IDF"], writes=[f"pb{bk}"])
                    S.emit("dve", (lambda TM=TM, j=j, bk=bk: lambda e: e.tensor_copy(
                        TM[:, j * 128:(j + 1) * 128], PB[:, bk * 512:bk * 512 + 128]))(),
                        reads=[f"pb{bk}"], writes=[tk])
                S.emit("sp", (lambda TM=TM, r0=r0: lambda e: e.dma_start(
                    out=y_out[r0:r0 + 128, :], in_=TM))(),
                    reads=[tk], sem="st", dma=True)

        LNB = [(6, 7, 4, 5), (2, 3, 0, 1)]

        def ln_sums(s, v):
            hk = f"XH{s}"
            XHs = XH[s]
            bsx, bsq, _, _ = LNB[v]
            S.emit("dve", (lambda s=s, XHs=XHs: lambda e: e.tensor_copy(XHs[:, :], X[s][:, :]))(),
                   reads=[f"X{s}"], writes=[hk])
            for j in range(NCH):
                sq = (j % 2) + 2 * v
                S.emit("act", (lambda j=j, sq=sq, XHs=XHs: lambda e: e.activation(
                    SQ[sq][:, :], XHs[:, j * 512:(j + 1) * 512], AF.Square))(),
                    reads=[hk], writes=[f"SQ{sq}"])
                S.emit("pe", (lambda j=j, XHs=XHs, bsx=bsx: lambda e: e.matmul(
                    PB[0:1, bsx * 512:bsx * 512 + 512], ONESB[:, :], XHs[:, j * 512:(j + 1) * 512],
                    start=(j == 0), stop=(j == NCH - 1)))(),
                    reads=[hk, "ONESB"], writes=[f"pb{bsx}"])
                S.emit("pe", (lambda j=j, sq=sq, bsq=bsq: lambda e: e.matmul(
                    PB[0:1, bsq * 512:bsq * 512 + 512], ONESB[:, :], SQ[sq][:, :],
                    start=(j == 0), stop=(j == NCH - 1)))(),
                    reads=[f"SQ{sq}", "ONESB"], writes=[f"pb{bsq}"])

        def ln_rows(s, v):
            bsx, bsq, bm, br = LNB[v]
            SX, RA, RB, RC = SXRv[v], ROWAv[v], ROWBv[v], ROWCv[v]
            kx, ka, kb, kc = "SXR", "ROWA", "ROWB", "ROWC"
            S.emit("dve", (lambda: lambda e: e.tensor_copy(SX[:, 0:512], PB[0:1, bsx * 512:bsx * 512 + 512]))(),
                   reads=[f"pb{bsx}"], writes=[kx])
            S.emit("dve", (lambda: lambda e: e.tensor_copy(SX[:, 512:1024], PB[0:1, bsq * 512:bsq * 512 + 512]))(),
                   reads=[f"pb{bsq}"], writes=[kx])
            S.emit("dve", (lambda: lambda e: e.tensor_scalar_mul(RA[:, :], SX[:, 0:512], 1.0 / 2048.0))(),
                   reads=[kx], writes=[ka])
            S.emit("dve", (lambda: lambda e: e.tensor_mul(RB[:, :], RA[:, :], RA[:, :]))(),
                   reads=[ka], writes=[kb])
            S.emit("dve", (lambda: lambda e: e.scalar_tensor_tensor(
                RC[:, :], SX[:, 512:1024], 1.0 / 2048.0, RB[:, :], ALU.mult, ALU.subtract))(),
                reads=[kx, kb], writes=[kc])
            S.emit("act", (lambda: lambda e: e.activation(RB[:, :], RC[:, :], AF.Ln, bias=EPSC[0:1, 0:1]))(),
                   reads=[kc, "EPSC"], writes=[kb])
            S.emit("act", (lambda: lambda e: e.activation(RA[:, :], RB[:, :], AF.Exp, scale=-0.5))(),
                   reads=[kb], writes=[ka])
            S.emit("pe", (lambda: lambda e: e.matmul(pb(bm), CP[0:1, 128:256], SX[0:1, 0:512],
                                                     start=True, stop=True))(),
                   reads=[kx, "CP"], writes=[f"pb{bm}"])
            S.emit("pe", (lambda: lambda e: e.matmul(pb(br), CP[0:1, 0:128], RA[0:1, :],
                                                     start=True, stop=True))(),
                   reads=[ka, "CP"], writes=[f"pb{br}"])

        def ln_apply(s, v):
            _, _, bm, br = LNB[v]
            hk = f"XH{s}"
            for j in range(NCH):
                t = (j % 2) + 2 * v
                S.emit("dve", (lambda s=s, j=j, t=t, bm=bm: lambda e: e.tensor_add(
                    T1[t][:, :], X[s][:, j * 512:(j + 1) * 512], pb(bm)))(),
                    reads=[f"X{s}", f"pb{bm}"], writes=[f"T1{t}"])
                S.emit("dve", (lambda s=s, j=j, t=t, br=br: lambda e: e.tensor_mul(
                    XH[s][:, j * 512:(j + 1) * 512], T1[t][:, :], pb(br)))(),
                    reads=[f"T1{t}", f"pb{br}"], writes=[hk])

        def layernorm(s, v=0):
            ln_sums(s, v)
            ln_rows(s, v)
            ln_apply(s, v)

        def ln_pair():
            ln_sums(0, 0)
            ln_sums(1, 1)
            ln_rows(0, 0)
            ln_rows(1, 1)
            ln_apply(0, 0)
            ln_apply(1, 1)

        def hkey(s, k):
            c0 = s * 16384 + k * 512
            return ht_keys(c0, c0 + 512)

        def ffn_pair(L):
            nI = NI[L]
            gu, dn = dr[f"gu{L}"], dr[f"dn{L}"]
            sgc, suc = C_SG[L], C_SU[L]
            for (h0, hlen) in HALVES[L]:
                for k in range(hlen):
                    i = h0 + k
                    sl = k % 2
                    wk = f"WBUF{sl}"
                    bk = (k % 2) * 4
                    S.emit("pool", (lambda i=i, sl=sl: lambda e: e.dma_start(
                        out=WB[sl][:, :].rearrange("p (w c f) -> p w c f", w=2, c=NCH),
                        in_=gu[i].rearrange("w c p f -> p w c f")))(),
                        writes=[wk], sem=f"w{sl}", dma=True)
                    S.emit("dve", (lambda sl=sl: lambda e: e.tensor_scalar_add(
                        WB[sl][:, :], WB[sl][:, :], -128.0))(),
                        reads=[wk], writes=[wk])
                    S.emit("pe", [(lambda sl=sl, c=c, s=s, bk=bk: lambda e: e.matmul(
                        pb(bk + 2 * s), WB[sl][:, c * 128:(c + 1) * 128],
                        XH[s][:, c * 512:(c + 1) * 512], start=(c == 0), stop=(c == NCH - 1)))()
                        for c in range(NCH) for s in range(2)],
                        reads=[wk, "XH0", "XH1"], writes=[f"pb{bk}", f"pb{bk + 2}"])
                    S.emit("pe", [(lambda sl=sl, c=c, s=s, bk=bk: lambda e: e.matmul(
                        pb(bk + 1 + 2 * s), WB[sl][:, (NCH + c) * 128:(NCH + c + 1) * 128],
                        XH[s][:, c * 512:(c + 1) * 512], start=(c == 0), stop=(c == NCH - 1)))()
                        for c in range(NCH) for s in range(2)],
                        reads=[wk, "XH0", "XH1"], writes=[f"pb{bk + 1}", f"pb{bk + 3}"])
                    for s in range(2):
                        S.emit("act", (lambda i=i, s=s, bk=bk: lambda e: e.activation(
                            SG[s][:, :], pb(bk + 2 * s), AF.Silu, scale=CS[:, sgc + i:sgc + i + 1]))(),
                            reads=[f"pb{bk + 2 * s}", "CS"], writes=[f"SG{s}"])
                        S.emit("dve", (lambda i=i, s=s, k=k, bk=bk: lambda e: e.scalar_tensor_tensor(
                            HT[:, s * 16384 + k * 512:s * 16384 + (k + 1) * 512],
                            pb(bk + 1 + 2 * s), CS[:, suc + i:suc + i + 1],
                            SG[s][:, :], ALU.mult, ALU.mult))(),
                            reads=[f"pb{bk + 1 + 2 * s}", f"SG{s}", "CS"],
                            writes=hkey(s, k))
                for q in range(4):
                    cbs = [(h0 + a, min(8, hlen - a)) for a in range(0, hlen, 8)]
                    for (c0, bs) in cbs:
                        sl = (c0 // 8) % 2
                        wk = f"WBUF{sl}"
                        S.emit("pool", (lambda q=q, c0=c0, bs=bs, sl=sl: lambda e: e.dma_start(
                            out=WB[sl][:, 0:bs * 512].rearrange("p (c f) -> p c f", c=bs),
                            in_=dn[q, c0:c0 + bs].rearrange("c p f -> p c f")))(),
                            writes=[wk], sem=f"w{sl}", dma=True)
                        S.emit("dve", (lambda bs=bs, sl=sl: lambda e: e.tensor_scalar_add(
                            WB[sl][:, 0:bs * 512], WB[sl][:, 0:bs * 512], -128.0))(),
                            reads=[wk], writes=[wk])
                        for c4 in range(bs):
                            c = c0 + c4
                            kin = c - h0
                            S.emit("pe", [(lambda sl=sl, c4=c4, c=c, kin=kin, s=s, o=o, h0=h0, hlen=hlen: lambda e: e.matmul(
                                pb(s * 4 + o), WB[sl][:, c4 * 512 + o * 128:c4 * 512 + (o + 1) * 128],
                                HT[:, s * 16384 + kin * 512:s * 16384 + (kin + 1) * 512],
                                start=(c == h0), stop=(c == h0 + hlen - 1)))()
                                for o in range(4) for s in range(2)],
                                reads=[wk] + hkey(0, kin) + hkey(1, kin),
                                writes=[f"pb{b}" for b in range(8)])
                    for s in range(2):
                        for o in range(4):
                            ho = 4 * q + o
                            S.emit("dve", (lambda s=s, L=L, ho=ho, o=o: lambda e: e.scalar_tensor_tensor(
                                X[s][:, ho * 512:(ho + 1) * 512], pb(s * 4 + o),
                                HS[:, 16 * L + ho:16 * L + ho + 1],
                                X[s][:, ho * 512:(ho + 1) * 512], ALU.mult, ALU.add))(),
                                reads=[f"pb{s * 4 + o}", f"X{s}", "HS"], writes=[f"X{s}"])

        def att0(s):
            xk = f"X{s}"
            hk = f"XH{s}"
            for w in range(3):
                for eo in range(NCH):
                    sl = eo % 2
                    wk = f"WBUF{sl}"
                    bk = eo % 2
                    S.emit("pool", (lambda w=w, eo=eo, sl=sl: lambda e: e.dma_start(
                        out=WB[sl][:, 0:2048].rearrange("p (c f) -> p c f", c=NCH),
                        in_=dr["w3"][w, eo].rearrange("c p f -> p c f")))(),
                        writes=[wk], sem=f"w{sl}", dma=True)
                    S.emit("pe", [(lambda sl=sl, c=c, bk=bk: lambda e: e.matmul(
                        pb(bk), WB[sl][:, c * 128:(c + 1) * 128],
                        XH[s][:, c * 512:(c + 1) * 512], start=(c == 0), stop=(c == NCH - 1)))()
                        for c in range(NCH)],
                        reads=[wk, hk], writes=[f"pb{bk}"])
                    if w == 0:
                        S.emit("act", (lambda eo=eo, bk=bk: lambda e: e.copy(
                            QT[:, eo * 512:(eo + 1) * 512], pb(bk)))(),
                            reads=[f"pb{bk}"], writes=ht_keys(eo * 512, (eo + 1) * 512))
                    elif w == 1:
                        S.emit("dve", (lambda eo=eo, bk=bk: lambda e: e.tensor_copy(
                            KT[:, eo * 512:(eo + 1) * 512], pb(bk)))(),
                            reads=[f"pb{bk}"], writes=ht_keys(8192 + eo * 512, 8192 + (eo + 1) * 512))
                    else:
                        vs = eo % 2
                        S.emit("dve", (lambda eo=eo, bk=bk, vs=vs: lambda e: e.tensor_copy(
                            VFS[vs][:, :], pb(bk)))(),
                            reads=[f"pb{bk}"], writes=[f"SG{vs}"])
                        for t4 in range(4):
                            m = eo * 4 + t4
                            tb = 2 + t4 % 2
                            S.emit("pe", (lambda vs=vs, t4=t4, tb=tb: lambda e: e.transpose(
                                PB[:, tb * 512:tb * 512 + 256].bitcast(BF16)[:, 0:128],
                                VFS[vs][:, t4 * 128:(t4 + 1) * 128], IDB[:, :]))(),
                                reads=[f"SG{vs}", "IDB"], writes=[f"pb{tb}"])
                            S.emit("dve", (lambda m=m, tb=tb: lambda e: e.tensor_copy(
                                VT[:, m * 128:(m + 1) * 128],
                                PB[:, tb * 512:tb * 512 + 256].bitcast(BF16)[:, 0:128]))(),
                                reads=[f"pb{tb}"],
                                writes=ht_keys(16384 + m * 128, 16384 + (m + 1) * 128))
            qk = ht_keys(0, 8192)
            kk = ht_keys(8192, 16384)
            vk = ht_keys(16384, 24576)
            for m in range(64):
                lv = KT[:, (m // 4) * 512 + (m % 4) * 128:(m // 4) * 512 + (m % 4) * 128 + 128]
                for b in range(4):
                    r0 = (b % 2) * 4
                    col = 4 * m + b
                    S.emit("pe", [(lambda lv=lv, b=b, k=k, r0=r0: lambda e: e.matmul(
                        pb(r0 + k), lv, QT[:, b * 2048 + k * 512:b * 2048 + (k + 1) * 512],
                        start=True, stop=True))() for k in range(4)],
                        reads=kk + qk, writes=pbk(r0, 4))
                    S.emit("dve", (lambda r0=r0: lambda e: e.reduce_max(
                        NM[:, :], PB[:, r0 * 512:r0 * 512 + 2048], axis=AX.X, negate=True))(),
                        reads=pbk(r0, 4), writes=["NM"])
                    S.emit("dve", (lambda col=col: lambda e: e.tensor_scalar_mul(
                        BSALL[:, col:col + 1], NM[:, :], ATT_SCALE))(),
                        reads=["NM"], writes=[f"BS{col % 8}"])
                    S.emit("act", (lambda r0=r0, col=col, b=b: lambda e: e.activation(
                        SC[b % 2][:, :], PB[:, r0 * 512:r0 * 512 + 2048], AF.Exp,
                        bias=BSALL[:, col:col + 1], scale=ATT_SCALE,
                        accum_out=LAALL[:, col:col + 1]))(),
                        reads=pbk(r0, 4) + [f"BS{col % 8}"], writes=SCK[b % 2] + [f"LA{col % 8}"])
            bsr = BSALL.rearrange("p (m b) -> p m b", b=4)
            dr3 = DALL.rearrange("p (m b) -> p m b", b=4)
            all_bs = [f"BS{i}" for i in range(8)]
            all_la = [f"LA{i}" for i in range(8)]
            S.emit("dve", lambda e: e.tensor_tensor(B_ALL[:, :], bsr[:, :, 0], bsr[:, :, 1], ALU.min),
                   reads=all_bs, writes=["B_ALL"])
            S.emit("dve", lambda e: e.tensor_tensor(B_ALL[:, :], B_ALL[:, :], bsr[:, :, 2], ALU.min),
                   reads=all_bs + ["B_ALL"], writes=["B_ALL"])
            S.emit("dve", lambda e: e.tensor_tensor(B_ALL[:, :], B_ALL[:, :], bsr[:, :, 3], ALU.min),
                   reads=all_bs + ["B_ALL"], writes=["B_ALL"])
            bmin = B_ALL.rearrange("p (m o) -> p m o", o=1)
            S.emit("dve", lambda e: e.tensor_sub(
                dr3, bsr, bmin.broadcast_to([128, 64, 4])),
                reads=all_bs + ["B_ALL"], writes=["DALL"])
            S.emit("act", lambda e: e.activation(FALL[:, :], DALL[:, :], AF.Exp, scale=-1.0),
                   reads=["DALL"], writes=["FALL"])
            S.emit("dve", lambda e: e.tensor_mul(PALL[:, :], FALL[:, :], LAALL[:, :]),
                   reads=["FALL"] + all_la, writes=["PALL"])
            p3 = PALL.rearrange("p (m b) -> p m b", b=4)
            S.emit("dve", lambda e: e.tensor_add(L_ALL[:, :], p3[:, :, 0], p3[:, :, 1]),
                   reads=["PALL"], writes=["L_ALL"])
            S.emit("dve", lambda e: e.tensor_add(L_ALL[:, :], L_ALL[:, :], p3[:, :, 2]),
                   reads=["PALL", "L_ALL"], writes=["L_ALL"])
            S.emit("dve", lambda e: e.tensor_add(L_ALL[:, :], L_ALL[:, :], p3[:, :, 3]),
                   reads=["PALL", "L_ALL"], writes=["L_ALL"])
            S.emit("act", lambda e: e.activation(LNL[:, :], L_ALL[:, :], AF.Ln),
                   reads=["L_ALL"], writes=["LNL"])
            S.emit("dve", lambda e: e.tensor_sub(BIASV[:, :], B_ALL[:, :], LNL[:, :]),
                   reads=["B_ALL", "LNL"], writes=["BIASV"])
            for Eb in range(8):
                def emit_sv(m):
                    sc_t = m % 2
                    S.emit("pe", [(lambda m=m, k=k, sc_t=sc_t: lambda e: e.matmul(
                        pb(4 + k), VT[:, m * 128:(m + 1) * 128],
                        SC[sc_t][:, k * 512:(k + 1) * 512],
                        start=(m == 0), stop=(m == 63)))() for k in range(2)],
                        reads=vk + SCK[sc_t], writes=pbk(4, 2))

                for m in range(64):
                    lv = KT[:, (m // 4) * 512 + (m % 4) * 128:(m // 4) * 512 + (m % 4) * 128 + 128]
                    r0 = (m % 2) * 2
                    S.emit("pe", [(lambda lv=lv, Eb=Eb, k=k, r0=r0: lambda e: e.matmul(
                        pb(r0 + k), lv, QT[:, Eb * 1024 + k * 512:Eb * 1024 + (k + 1) * 512],
                        start=True, stop=True))() for k in range(2)],
                        reads=kk + qk, writes=pbk(r0, 2))
                    if m > 0:
                        emit_sv(m - 1)
                    S.emit("act", (lambda r0=r0, m=m: lambda e: e.activation(
                        SC[m % 2][:, 0:1024], PB[:, r0 * 512:r0 * 512 + 1024], AF.Exp,
                        bias=BIASV[:, m:m + 1], scale=ATT_SCALE))(),
                        reads=pbk(r0, 2) + ["BIASV"], writes=SCK[m % 2])
                emit_sv(63)
                for k in range(2):
                    j = 2 * Eb + k
                    S.emit("dve", (lambda s=s, j=j, k=k: lambda e: e.tensor_add(
                        X[s][:, j * 512:(j + 1) * 512],
                        X[s][:, j * 512:(j + 1) * 512], pb(4 + k)))(),
                        reads=[f"pb{4 + k}", xk], writes=[xk])

        def att1(s):
            xk = f"X{s}"
            hk = f"XH{s}"
            for eo in range(24):
                sl = eo % 2
                wk = f"WBUF{sl}"
                bk = eo % 2
                S.emit("pool", (lambda eo=eo, sl=sl: lambda e: e.dma_start(
                    out=WB[sl][:, 0:2048].rearrange("p (c f) -> p c f", c=NCH),
                    in_=dr["qkv1"][eo].rearrange("c p f -> p c f")))(),
                    writes=[wk], sem=f"w{sl}", dma=True)
                S.emit("pe", [(lambda sl=sl, c=c, bk=bk: lambda e: e.matmul(
                    pb(bk), WB[sl][:, c * 128:(c + 1) * 128],
                    XH[s][:, c * 512:(c + 1) * 512], start=(c == 0), stop=(c == NCH - 1)))()
                    for c in range(NCH)],
                    reads=[wk, hk], writes=[f"pb{bk}"])
                if eo < 20:
                    S.emit("dve", (lambda eo=eo, bk=bk: lambda e: e.tensor_scalar_add(
                        SCRAP[:, :], pb(bk), CS[:, C_QB1 + eo:C_QB1 + eo + 1]))(),
                        reads=[f"pb{bk}", "CS"], writes=["HT7"])
                else:
                    g = eo - 20
                    S.emit("dve", (lambda eo=eo, g=g, bk=bk: lambda e: e.tensor_scalar_add(
                        VCH[:, g * 512:(g + 1) * 512], pb(bk), CS[:, C_QB1 + eo:C_QB1 + eo + 1]))(),
                        reads=[f"pb{bk}", "CS"], writes=VCH_KEYS)
            for eo2 in range(NCH):
                sl = eo2 % 2
                wk = f"WBUF{sl}"
                bk = 2 + eo2 % 2
                S.emit("pool", (lambda eo2=eo2, sl=sl: lambda e: e.dma_start(
                    out=WB[sl][:, 0:2048].rearrange("p (c f) -> p c f", c=NCH),
                    in_=dr["ow1"][eo2].rearrange("c p f -> p c f")))(),
                    writes=[wk], sem=f"w{sl}", dma=True)
                S.emit("pe", [(lambda sl=sl, c=c, bk=bk: lambda e: e.matmul(
                    pb(bk), WB[sl][:, c * 128:(c + 1) * 128],
                    VCH[:, (c // 4) * 512:(c // 4 + 1) * 512],
                    start=(c == 0), stop=(c == NCH - 1)))() for c in range(NCH)],
                    reads=[wk] + VCH_KEYS, writes=[f"pb{bk}"])
                S.emit("dve", (lambda s=s, eo2=eo2, bk=bk: lambda e: e.scalar_tensor_tensor(
                    X[s][:, eo2 * 512:(eo2 + 1) * 512], pb(bk),
                    CS[:, C_OB1 + eo2:C_OB1 + eo2 + 1],
                    X[s][:, eo2 * 512:(eo2 + 1) * 512], ALU.add, ALU.add))(),
                    reads=[f"pb{bk}", xk, "CS"], writes=[xk])

        for s in (0, 1):
            inload(s)
        for idx in seg:
            kind, L = SUBLAYERS[idx]
            if kind == "ffn":
                layernorm(0, 0)
                layernorm(1, 1)
                ffn_pair(L)
            else:
                for s in (0, 1):
                    layernorm(s, 0)
                    if kind == "att0":
                        att0(s)
                    else:
                        att1(s)
        for s in (0, 1):
            outstore(s)
        S.wait_only("sp", [("st", S.cnt["st"])])

        for q in QUEUES:
            for waits, _, _, _ in S.tasks[q]:
                for ps, c in waits:
                    assert c <= S.cnt[ps], (q, ps, c, S.cnt[ps])

        def runner(qname):
            def body(eng):
                for waits, fns, sem, step in S.tasks[qname]:
                    for ps, c in waits:
                        eng.wait_ge(sems[ps], c)
                    inst = None
                    for f in fns:
                        inst = f(eng)
                    if inst is not None and step:
                        inst.then_inc(sems[sem], step)
            return body

        block.tensor(runner("pe"))
        block.scalar(runner("act"))
        block.vector(runner("dve"))
        block.gpsimd(runner("pool"))
        block.sync(runner("sp"))

    return nc, names


def _pack_lhsT_in_out(w):
    nin, nout = w.shape
    a = w.reshape(nin // 128, 128, nout // 128, 128)
    return np.ascontiguousarray(a.transpose(2, 0, 1, 3))


def _pack_gu(gw, uw):
    nI = gw.shape[0] // 128
    def pk(w):
        return w.reshape(nI, 128, NCH, 128).transpose(0, 2, 3, 1)
    return np.ascontiguousarray(np.stack([pk(gw), pk(uw)], axis=1))


def _pack_down(dw):
    nI = dw.shape[1] // 128
    a = dw.reshape(4, 4, 128, nI, 128)
    return np.ascontiguousarray(a.transpose(0, 3, 4, 1, 2).reshape(4, nI, 128, 512))


def _pack_consts(i):
    cs = np.zeros((128, C_NCOLS), np.float32)
    cs[:, C_SG[0]:C_SG[0] + 64] = np.asarray(i["L0_ffn_gate_scale"], np.float32).reshape(64, 128).T
    cs[:, C_SU[0]:C_SU[0] + 64] = np.asarray(i["L0_ffn_up_scale"], np.float32).reshape(64, 128).T
    cs[:, C_SD[0]:C_SD[0] + 16] = np.asarray(i["L0_ffn_down_scale"], np.float32).reshape(16, 128).T
    cs[:, C_SG[1]:C_SG[1] + 44] = np.asarray(i["L1_ffn_s_gate"], np.float32).reshape(44, 128).T
    cs[:, C_SU[1]:C_SU[1] + 44] = np.asarray(i["L1_ffn_s_up"], np.float32).reshape(44, 128).T
    cs[:, C_SD[1]:C_SD[1] + 16] = np.asarray(i["L1_ffn_s_down"], np.float32).reshape(16, 128).T
    cs[:, C_QB1:C_QB1 + 24] = np.asarray(i["L1_attn_qkv_b"], np.float32).reshape(24, 128).T
    cs[:, C_OB1:C_OB1 + 16] = np.asarray(i["L1_attn_out_b"], np.float32).reshape(16, 128).T
    return cs


def _pack_all(i):
    p = {}
    p["cs"] = _pack_consts(i)
    cp = np.zeros((1, 256), np.float32)
    cp[0, 0:128] = 1.0
    cp[0, 128:256] = -1.0 / 2048.0
    p["cp"] = cp
    p["idb"] = np.eye(128, dtype=ml_dtypes.bfloat16)
    p["idf"] = np.eye(128, dtype=np.float32)
    p["gu0"] = _pack_gu(np.asarray(i["L0_ffn_gate_w"]), np.asarray(i["L0_ffn_up_w"]))
    p["dn0"] = _pack_down(np.asarray(i["L0_ffn_down_w"]))
    p["gu1"] = _pack_gu(np.asarray(i["L1_ffn_wq_gate"]), np.asarray(i["L1_ffn_wq_up"]))
    p["dn1"] = _pack_down(np.asarray(i["L1_ffn_wq_down"]))
    p["w3"] = np.ascontiguousarray(np.stack([
        _pack_lhsT_in_out(np.asarray(i["L0_attn_w_query"], np.float32)),
        _pack_lhsT_in_out(np.asarray(i["L0_attn_w_key"], np.float32)),
        _pack_lhsT_in_out(np.asarray(i["L0_attn_w_value"], np.float32))]))
    p["qkv1"] = _pack_lhsT_in_out(np.asarray(i["L1_attn_qkv_w"], np.float32))
    p["ow1"] = _pack_lhsT_in_out(np.asarray(i["L1_attn_out_w"], np.float32))
    return p


def kernel(**inputs):
    packed = _pack_all(inputs)
    cur = np.ascontiguousarray(np.asarray(inputs["L0_attn_x"], np.float32).reshape(8192, 2048))
    for seg in SEGMENTS:
        nc, names = build_nc(seg)
        in_maps = []
        for c in range(8):
            m = {"x": np.ascontiguousarray(cur[c * 1024:(c + 1) * 1024])}
            for n in names:
                m[n] = packed[n]
            in_maps.append(m)
        res = run_bass_kernel_spmd(nc, in_maps, core_ids=list(range(8)))
        cur = np.concatenate([r["y"] for r in res.results], axis=0)
    return cur.reshape(1, 8192, 2048).astype(np.float32)


# version 05
# speedup vs baseline: 1.0941x; 1.0941x over previous
import numpy as np
import ml_dtypes
import concourse.bass as bass
import concourse.mybir as mybir
from concourse.bass_utils import run_bass_kernel_spmd

F32 = mybir.dt.float32
BF16 = mybir.dt.bfloat16
I32 = mybir.dt.int32
AF = mybir.ActivationFunctionType
ALU = mybir.AluOpType
AX = mybir.AxisListType

NCH = 16
T = 512
NI = {0: 64, 1: 44}
HALVES = {0: [(0, 32), (32, 32)], 1: [(0, 24), (24, 20)]}
ATT_SCALE = float(1.0 / np.sqrt(128.0))
EPS = 1e-5

C_SG = {0: 0, 1: 144}
C_SU = {0: 64, 1: 188}
C_SD = {0: 128, 1: 232}
C_QB1 = 248
C_OB1 = 272
C_NCOLS = 288

SUBLAYERS = [("ffn", 0), ("att0", 0), ("ffn", 0), ("ffn", 1), ("att1", 1), ("ffn", 1)]
SEGMENTS = [[0, 1, 2, 3, 4, 5]]

SEM_NAMES = ["pe", "act", "dve", "pool", "sp", "w0", "w1", "io0", "io1", "cst", "st"]
QUEUES = ["pe", "act", "dve", "pool", "sp"]


class Sched:

    def __init__(self):
        self.tasks = {q: [] for q in QUEUES}
        self.cnt = {s: 0 for s in SEM_NAMES}
        self.floor = {}
        self.res = {}

    def emit(self, q, fns, reads=(), writes=(), sem=None, dma=False):
        if not isinstance(fns, (list, tuple)):
            fns = [fns]
        sem = sem or q
        step = 16 if dma else 1
        need = {}

        def add(w):
            if w is not None:
                need[w[0]] = max(need.get(w[0], 0), w[1])

        for k in reads:
            e = self.res.get(k)
            if e is not None:
                add(e[0])
        for k in writes:
            e = self.res.get(k)
            if e is not None:
                add(e[0])
                for r in e[1]:
                    add(r)
        waits = []
        for ps in sorted(need):
            c = need[ps]
            key = (q, ps)
            if self.floor.get(key, 0) >= c:
                continue
            waits.append((ps, c))
            self.floor[key] = c
        self.cnt[sem] += step
        mc = (sem, self.cnt[sem])
        self.tasks[q].append((waits, fns, sem, step))
        for k in reads:
            self.res.setdefault(k, [None, []])[1].append(mc)
        for k in writes:
            self.res[k] = [mc, []]

    def wait_only(self, q, sems_counts):
        self.tasks[q].append((list(sems_counts), [], None, 0))


def ht_keys(c0, c1):
    return [f"HT{k}" for k in range(c0 // 4096, (c1 - 1) // 4096 + 1)]


def build_nc(seg):
    kinds = {SUBLAYERS[i][0] for i in seg}
    layers = {SUBLAYERS[i][1] for i in seg if SUBLAYERS[i][0] == "ffn"}

    nc = bass.Bass("TRN2", target_bir_lowering=False)
    x_in = nc.dram_tensor("x", [1024, 2048], F32, kind="ExternalInput").ap()
    y_out = nc.dram_tensor("y", [1024, 2048], F32, kind="ExternalOutput").ap()
    cs_in = nc.dram_tensor("cs", [128, C_NCOLS], F32, kind="ExternalInput").ap()
    cp_in = nc.dram_tensor("cp", [1, 256], F32, kind="ExternalInput").ap()
    idb_in = nc.dram_tensor("idb", [128, 128], BF16, kind="ExternalInput").ap()
    idf_in = nc.dram_tensor("idf", [128, 128], F32, kind="ExternalInput").ap()
    names = ["cs", "cp", "idb", "idf"]
    dr = {}
    for L in layers:
        dr[f"gu{L}"] = nc.dram_tensor(f"gu{L}", [NI[L], 2, NCH, 128, 128], I32, kind="ExternalInput").ap()
        dr[f"dn{L}"] = nc.dram_tensor(f"dn{L}", [4, NI[L], 128, 512], I32, kind="ExternalInput").ap()
        names += [f"gu{L}", f"dn{L}"]
    if "att0" in kinds:
        dr["w3"] = nc.dram_tensor("w3", [3, NCH, NCH, 128, 128], F32, kind="ExternalInput").ap()
        names += ["w3"]
    if "att1" in kinds:
        dr["qkv1"] = nc.dram_tensor("qkv1", [24, NCH, 128, 128], F32, kind="ExternalInput").ap()
        dr["ow1"] = nc.dram_tensor("ow1", [NCH, NCH, 128, 128], F32, kind="ExternalInput").ap()
        names += ["qkv1", "ow1"]

    S = Sched()

    from contextlib import ExitStack
    with ExitStack() as st_:
        def sb(name, shape, dt):
            return st_.enter_context(nc.sbuf_tensor(name, shape, dt))

        X = [sb("X0", [128, NCH * T], F32), sb("X1", [128, NCH * T], F32)]
        XH = [sb("XHa", [128, NCH * T], BF16), sb("XHb", [128, NCH * T], BF16)]
        HT = sb("HT", [128, 64 * T], BF16)
        WB = [sb("WB0", [128, 4096], BF16), sb("WB1", [128, 4096], BF16)]
        SG = [sb("SGa", [128, T], BF16), sb("SGb", [128, T], BF16)]
        VFS = SG
        T1 = [sb(f"T1{i}", [128, T], F32) for i in range(4)]
        SQ = [sb(f"SQ{i}", [128, T], BF16) for i in range(4)]
        CS = sb("CS", [128, C_NCOLS], F32)
        CP = sb("CP", [1, 256], F32)
        IDB = sb("IDB", [128, 128], BF16)
        IDF = sb("IDF", [128, 128], F32)
        ONESB = sb("ONESB", [128, 1], BF16)
        EPSC = sb("EPSC", [128, 1], F32)
        HS = sb("HS", [128, 32], F32)
        SXR = sb("SXR", [1, 1024], F32)
        ROWA = sb("ROWA", [1, T], F32)
        ROWB = sb("ROWB", [1, T], F32)
        ROWC = sb("ROWC", [1, T], F32)
        SXRv = [SXR, SXR]
        ROWAv = [ROWA, ROWA]
        ROWBv = [ROWB, ROWB]
        ROWCv = [ROWC, ROWC]
        NM = sb("NM", [128, 1], F32)
        BSALL = sb("BSALL", [128, 256], F32)
        LAALL = sb("LAALL", [128, 256], F32)
        DALL = sb("DALL", [128, 256], F32)
        FALL = DALL
        PALL = DALL
        L_ALL = sb("L_ALL", [128, 64], F32)
        B_ALL = sb("B_ALL", [128, 64], F32)
        LNL = sb("LNL", [128, 64], F32)
        BIASV = sb("BIASV", [128, 64], F32)
        PB = st_.enter_context(nc.psum_tensor("PB", [128, 4096], F32))
        sems = {n: st_.enter_context(nc.semaphore(n)) for n in SEM_NAMES}
        block = st_.enter_context(nc.Block())

        def pb(b, n=1):
            return PB[:, b * 512:(b + n) * 512]

        def pbk(b, n=1):
            return [f"pb{i}" for i in range(b, b + n)]

        HTF = HT[:, :].bitcast(F32)
        QT = HT[:, 0:8192]
        KT = HT[:, 8192:16384]
        VT = HT[:, 16384:24576]
        VCH = HT[:, 24576:26624]
        VCH_KEYS = ht_keys(24576, 26624)
        SC = [HT[:, 26624:28672], HT[:, 28672:30720]]
        SCK = [["HT6"], ["HT7"]]
        SC4 = [HT[:, 26624 + 1024 * i:26624 + 1024 * (i + 1)] for i in range(4)]
        SC4K = [["HT6"], ["HT6"], ["HT7"], ["HT7"]]
        SCRAP = HT[:, 30720:31232]

        S.emit("sp", lambda e: e.dma_start(out=CS[:, :], in_=cs_in), writes=["CS"], sem="cst", dma=True)
        S.emit("sp", lambda e: e.dma_start(out=CP[:, :], in_=cp_in), writes=["CP"], sem="cst", dma=True)
        S.emit("sp", lambda e: e.dma_start(out=IDB[:, :], in_=idb_in), writes=["IDB"], sem="cst", dma=True)
        S.emit("sp", lambda e: e.dma_start(out=IDF[:, :], in_=idf_in), writes=["IDF"], sem="cst", dma=True)
        S.emit("dve", lambda e: e.memset(ONESB[:, :], 1.0), writes=["ONESB"])
        S.emit("dve", lambda e: e.memset(EPSC[:, :], EPS), writes=["EPSC"])
        for L in layers:
            c0, h0 = C_SD[L], 16 * L
            S.emit("dve", (lambda c0=c0, h0=h0: lambda e: e.tensor_scalar_mul(
                HS[:, h0:h0 + 16], CS[:, c0:c0 + 16], 0.5))(),
                reads=["CS"], writes=["HS"])

        def inload(s):
            for g in range(4):
                slot = g % 2
                TM = HTF[:, slot * 2048:(slot + 1) * 2048]
                tk = f"HT{slot}"
                r0 = s * 512 + g * 128
                S.emit("sp", (lambda TM=TM, r0=r0: lambda e: e.dma_start(
                    out=TM, in_=x_in[r0:r0 + 128, :]))(),
                    writes=[tk], sem=f"io{slot}", dma=True)
                for j in range(NCH):
                    bk = j % 4
                    S.emit("pe", (lambda TM=TM, j=j, bk=bk: lambda e: e.transpose(
                        PB[:, bk * 512:bk * 512 + 128], TM[:, j * 128:(j + 1) * 128], IDF[:, :]))(),
                        reads=[tk, "IDF"], writes=[f"pb{bk}"])
                    S.emit("dve", (lambda s=s, j=j, g=g, bk=bk: lambda e: e.tensor_copy(
                        X[s][:, j * 512 + g * 128:j * 512 + g * 128 + 128],
                        PB[:, bk * 512:bk * 512 + 128]))(),
                        reads=[f"pb{bk}"], writes=[f"X{s}"])

        def outstore(s):
            for g in range(4):
                slot = g % 2
                TM = HTF[:, slot * 2048:(slot + 1) * 2048]
                tk = f"HT{slot}"
                r0 = s * 512 + g * 128
                for j in range(NCH):
                    bk = j % 4
                    S.emit("pe", (lambda s=s, j=j, g=g, bk=bk: lambda e: e.transpose(
                        PB[:, bk * 512:bk * 512 + 128],
                        X[s][:, j * 512 + g * 128:j * 512 + g * 128 + 128], IDF[:, :]))(),
                        reads=[f"X{s}", "IDF"], writes=[f"pb{bk}"])
                    S.emit("dve", (lambda TM=TM, j=j, bk=bk: lambda e: e.tensor_copy(
                        TM[:, j * 128:(j + 1) * 128], PB[:, bk * 512:bk * 512 + 128]))(),
                        reads=[f"pb{bk}"], writes=[tk])
                S.emit("sp", (lambda TM=TM, r0=r0: lambda e: e.dma_start(
                    out=y_out[r0:r0 + 128, :], in_=TM))(),
                    reads=[tk], sem="st", dma=True)

        LNB = [(6, 7, 4, 5), (2, 3, 0, 1)]

        def ln_sums(s, v):
            hk = f"XH{s}"
            XHs = XH[s]
            bsx, bsq, _, _ = LNB[v]
            S.emit("dve", (lambda s=s, XHs=XHs: lambda e: e.tensor_copy(XHs[:, :], X[s][:, :]))(),
                   reads=[f"X{s}"], writes=[hk])
            S.emit("pe", [(lambda j=j, XHs=XHs, bsx=bsx: lambda e: e.matmul(
                PB[0:1, bsx * 512:bsx * 512 + 512], ONESB[:, :], XHs[:, j * 512:(j + 1) * 512],
                start=(j == 0), stop=(j == NCH - 1)))() for j in range(NCH)],
                reads=[hk, "ONESB"], writes=[f"pb{bsx}"])
            for j in range(NCH):
                sq = (j % 2) + 2 * v
                S.emit("act", (lambda j=j, sq=sq, XHs=XHs: lambda e: e.activation(
                    SQ[sq][:, :], XHs[:, j * 512:(j + 1) * 512], AF.Square))(),
                    reads=[hk], writes=[f"SQ{sq}"])
                S.emit("pe", (lambda j=j, sq=sq, bsq=bsq: lambda e: e.matmul(
                    PB[0:1, bsq * 512:bsq * 512 + 512], ONESB[:, :], SQ[sq][:, :],
                    start=(j == 0), stop=(j == NCH - 1)))(),
                    reads=[f"SQ{sq}", "ONESB"], writes=[f"pb{bsq}"])

        def ln_rows(s, v):
            bsx, bsq, bm, br = LNB[v]
            SX, RA, RB, RC = SXRv[v], ROWAv[v], ROWBv[v], ROWCv[v]
            kx, ka, kb, kc = "SXR", "ROWA", "ROWB", "ROWC"
            S.emit("dve", (lambda: lambda e: e.tensor_copy(SX[:, 0:512], PB[0:1, bsx * 512:bsx * 512 + 512]))(),
                   reads=[f"pb{bsx}"], writes=[kx])
            S.emit("dve", (lambda: lambda e: e.tensor_copy(SX[:, 512:1024], PB[0:1, bsq * 512:bsq * 512 + 512]))(),
                   reads=[f"pb{bsq}"], writes=[kx])
            S.emit("dve", (lambda: lambda e: e.tensor_scalar_mul(RA[:, :], SX[:, 0:512], 1.0 / 2048.0))(),
                   reads=[kx], writes=[ka])
            S.emit("dve", (lambda: lambda e: e.tensor_mul(RB[:, :], RA[:, :], RA[:, :]))(),
                   reads=[ka], writes=[kb])
            S.emit("dve", (lambda: lambda e: e.scalar_tensor_tensor(
                RC[:, :], SX[:, 512:1024], 1.0 / 2048.0, RB[:, :], ALU.mult, ALU.subtract))(),
                reads=[kx, kb], writes=[kc])
            S.emit("act", (lambda: lambda e: e.activation(RB[:, :], RC[:, :], AF.Ln, bias=EPSC[0:1, 0:1]))(),
                   reads=[kc, "EPSC"], writes=[kb])
            S.emit("act", (lambda: lambda e: e.activation(RA[:, :], RB[:, :], AF.Exp, scale=-0.5))(),
                   reads=[kb], writes=[ka])
            S.emit("pe", (lambda: lambda e: e.matmul(pb(bm), CP[0:1, 128:256], SX[0:1, 0:512],
                                                     start=True, stop=True))(),
                   reads=[kx, "CP"], writes=[f"pb{bm}"])
            S.emit("pe", (lambda: lambda e: e.matmul(pb(br), CP[0:1, 0:128], RA[0:1, :],
                                                     start=True, stop=True))(),
                   reads=[ka, "CP"], writes=[f"pb{br}"])

        def ln_apply(s, v):
            _, _, bm, br = LNB[v]
            hk = f"XH{s}"
            for j in range(NCH):
                t = (j % 2) + 2 * v
                S.emit("dve", (lambda s=s, j=j, t=t, bm=bm: lambda e: e.tensor_add(
                    T1[t][:, :], X[s][:, j * 512:(j + 1) * 512], pb(bm)))(),
                    reads=[f"X{s}", f"pb{bm}"], writes=[f"T1{t}"])
                S.emit("dve", (lambda s=s, j=j, t=t, br=br: lambda e: e.tensor_mul(
                    XH[s][:, j * 512:(j + 1) * 512], T1[t][:, :], pb(br)))(),
                    reads=[f"T1{t}", f"pb{br}"], writes=[hk])

        def layernorm(s, v=0):
            ln_sums(s, v)
            ln_rows(s, v)
            ln_apply(s, v)

        def ln_pair():
            ln_sums(0, 0)
            ln_sums(1, 1)
            ln_rows(0, 0)
            ln_rows(1, 1)
            ln_apply(0, 0)
            ln_apply(1, 1)

        def hkey(s, k):
            c0 = s * 16384 + k * 512
            return ht_keys(c0, c0 + 512)

        def ffn_pair(L):
            nI = NI[L]
            gu, dn = dr[f"gu{L}"], dr[f"dn{L}"]
            sgc, suc = C_SG[L], C_SU[L]
            pend = None
            dpend = None
            for (h0, hlen) in HALVES[L]:
                if dpend is not None:
                    dpend()
                    dpend = None
                for k in range(hlen):
                    i = h0 + k
                    sl = k % 2
                    wk = f"WBUF{sl}"
                    bk = (k % 2) * 4
                    S.emit("pool", (lambda i=i, sl=sl: lambda e: e.dma_start(
                        out=WB[sl][:, :].rearrange("p (w c f) -> p w c f", w=2, c=NCH),
                        in_=gu[i].rearrange("w c p f -> p w c f")))(),
                        writes=[wk], sem=f"w{sl}", dma=True)
                    S.emit("dve", (lambda sl=sl: lambda e: e.tensor_scalar_add(
                        WB[sl][:, :], WB[sl][:, :], -128.0))(),
                        reads=[wk], writes=[wk])
                    S.emit("pe", [(lambda sl=sl, c=c, s=s, bk=bk: lambda e: e.matmul(
                        pb(bk + 2 * s), WB[sl][:, c * 128:(c + 1) * 128],
                        XH[s][:, c * 512:(c + 1) * 512], start=(c == 0), stop=(c == NCH - 1)))()
                        for c in range(NCH) for s in range(2)],
                        reads=[wk, "XH0", "XH1"], writes=[f"pb{bk}", f"pb{bk + 2}"])
                    S.emit("pe", [(lambda sl=sl, c=c, s=s, bk=bk: lambda e: e.matmul(
                        pb(bk + 1 + 2 * s), WB[sl][:, (NCH + c) * 128:(NCH + c + 1) * 128],
                        XH[s][:, c * 512:(c + 1) * 512], start=(c == 0), stop=(c == NCH - 1)))()
                        for c in range(NCH) for s in range(2)],
                        reads=[wk, "XH0", "XH1"], writes=[f"pb{bk + 1}", f"pb{bk + 3}"])
                    def guevac(i=i, k=k, bk=bk):
                        for s in range(2):
                            S.emit("act", (lambda i=i, s=s, bk=bk: lambda e: e.activation(
                                SG[s][:, :], pb(bk + 2 * s), AF.Silu, scale=CS[:, sgc + i:sgc + i + 1]))(),
                                reads=[f"pb{bk + 2 * s}", "CS"], writes=[f"SG{s}"])
                            S.emit("dve", (lambda i=i, s=s, k=k, bk=bk: lambda e: e.scalar_tensor_tensor(
                                HT[:, s * 16384 + k * 512:s * 16384 + (k + 1) * 512],
                                pb(bk + 1 + 2 * s), CS[:, suc + i:suc + i + 1],
                                SG[s][:, :], ALU.mult, ALU.mult))(),
                                reads=[f"pb{bk + 1 + 2 * s}", f"SG{s}", "CS"],
                                writes=hkey(s, k))
                    if pend is not None:
                        pend()
                    pend = guevac
                if pend is not None:
                    pend()
                    pend = None
                for q in range(4):
                    cbs = [(h0 + a, min(8, hlen - a)) for a in range(0, hlen, 8)]
                    first_batch = True
                    for (c0, bs) in cbs:
                        sl = (c0 // 8) % 2
                        wk = f"WBUF{sl}"
                        S.emit("pool", (lambda q=q, c0=c0, bs=bs, sl=sl: lambda e: e.dma_start(
                            out=WB[sl][:, 0:bs * 512].rearrange("p (c f) -> p c f", c=bs),
                            in_=dn[q, c0:c0 + bs].rearrange("c p f -> p c f")))(),
                            writes=[wk], sem=f"w{sl}", dma=True)
                        S.emit("dve", (lambda bs=bs, sl=sl: lambda e: e.tensor_scalar_add(
                            WB[sl][:, 0:bs * 512], WB[sl][:, 0:bs * 512], -128.0))(),
                            reads=[wk], writes=[wk])
                        if first_batch and dpend is not None:
                            dpend()
                            dpend = None
                        first_batch = False
                        for c4 in range(bs):
                            c = c0 + c4
                            kin = c - h0
                            S.emit("pe", [(lambda sl=sl, c4=c4, c=c, kin=kin, s=s, o=o, h0=h0, hlen=hlen: lambda e: e.matmul(
                                pb(s * 4 + o), WB[sl][:, c4 * 512 + o * 128:c4 * 512 + (o + 1) * 128],
                                HT[:, s * 16384 + kin * 512:s * 16384 + (kin + 1) * 512],
                                start=(c == h0), stop=(c == h0 + hlen - 1)))()
                                for o in range(4) for s in range(2)],
                                reads=[wk] + hkey(0, kin) + hkey(1, kin),
                                writes=[f"pb{b}" for b in range(8)])
                    def devac(q=q):
                        for s in range(2):
                            for o in range(4):
                                ho = 4 * q + o
                                S.emit("dve", (lambda s=s, L=L, ho=ho, o=o: lambda e: e.scalar_tensor_tensor(
                                    X[s][:, ho * 512:(ho + 1) * 512], pb(s * 4 + o),
                                    HS[:, 16 * L + ho:16 * L + ho + 1],
                                    X[s][:, ho * 512:(ho + 1) * 512], ALU.mult, ALU.add))(),
                                    reads=[f"pb{s * 4 + o}", f"X{s}", "HS"], writes=[f"X{s}"])
                    dpend = devac
            if dpend is not None:
                dpend()

        def att0(s):
            xk = f"X{s}"
            hk = f"XH{s}"
            for w in range(3):
                for eo in range(NCH):
                    sl = eo % 2
                    wk = f"WBUF{sl}"
                    bk = (eo % 4) if w < 2 else (eo % 2)
                    S.emit("pool", (lambda w=w, eo=eo, sl=sl: lambda e: e.dma_start(
                        out=WB[sl][:, 0:2048].rearrange("p (c f) -> p c f", c=NCH),
                        in_=dr["w3"][w, eo].rearrange("c p f -> p c f")))(),
                        writes=[wk], sem=f"w{sl}", dma=True)
                    S.emit("pe", [(lambda sl=sl, c=c, bk=bk: lambda e: e.matmul(
                        pb(bk), WB[sl][:, c * 128:(c + 1) * 128],
                        XH[s][:, c * 512:(c + 1) * 512], start=(c == 0), stop=(c == NCH - 1)))()
                        for c in range(NCH)],
                        reads=[wk, hk], writes=[f"pb{bk}"])
                    if w == 0:
                        S.emit("act", (lambda eo=eo, bk=bk: lambda e: e.copy(
                            QT[:, eo * 512:(eo + 1) * 512], pb(bk)))(),
                            reads=[f"pb{bk}"], writes=ht_keys(eo * 512, (eo + 1) * 512))
                    elif w == 1:
                        S.emit("dve", (lambda eo=eo, bk=bk: lambda e: e.tensor_copy(
                            KT[:, eo * 512:(eo + 1) * 512], pb(bk)))(),
                            reads=[f"pb{bk}"], writes=ht_keys(8192 + eo * 512, 8192 + (eo + 1) * 512))
                    else:
                        vs = eo % 2
                        S.emit("dve", (lambda eo=eo, bk=bk, vs=vs: lambda e: e.tensor_copy(
                            VFS[vs][:, :], pb(bk)))(),
                            reads=[f"pb{bk}"], writes=[f"SG{vs}"])
                        for t4 in range(4):
                            m = eo * 4 + t4
                            tb = 2 + t4 % 2
                            S.emit("pe", (lambda vs=vs, t4=t4, tb=tb: lambda e: e.transpose(
                                PB[:, tb * 512:tb * 512 + 256].bitcast(BF16)[:, 0:128],
                                VFS[vs][:, t4 * 128:(t4 + 1) * 128], IDB[:, :]))(),
                                reads=[f"SG{vs}", "IDB"], writes=[f"pb{tb}"])
                            S.emit("dve", (lambda m=m, tb=tb: lambda e: e.tensor_copy(
                                VT[:, m * 128:(m + 1) * 128],
                                PB[:, tb * 512:tb * 512 + 256].bitcast(BF16)[:, 0:128]))(),
                                reads=[f"pb{tb}"],
                                writes=ht_keys(16384 + m * 128, 16384 + (m + 1) * 128))
            qk = ht_keys(0, 8192)
            kk = ht_keys(8192, 16384)
            vk = ht_keys(16384, 24576)
            for m in range(64):
                lv = KT[:, (m // 4) * 512 + (m % 4) * 128:(m // 4) * 512 + (m % 4) * 128 + 128]
                for b in range(4):
                    r0 = (b % 2) * 4
                    col = 4 * m + b
                    S.emit("pe", [(lambda lv=lv, b=b, k=k, r0=r0: lambda e: e.matmul(
                        pb(r0 + k), lv, QT[:, b * 2048 + k * 512:b * 2048 + (k + 1) * 512],
                        start=True, stop=True))() for k in range(4)],
                        reads=kk + qk, writes=pbk(r0, 4))
                    S.emit("dve", (lambda r0=r0: lambda e: e.reduce_max(
                        NM[:, :], PB[:, r0 * 512:r0 * 512 + 2048], axis=AX.X, negate=True))(),
                        reads=pbk(r0, 4), writes=["NM"])
                    S.emit("dve", (lambda col=col: lambda e: e.tensor_scalar_mul(
                        BSALL[:, col:col + 1], NM[:, :], ATT_SCALE))(),
                        reads=["NM"], writes=[f"BS{col % 8}"])
                    S.emit("act", (lambda r0=r0, col=col, b=b: lambda e: e.activation(
                        SC[b % 2][:, :], PB[:, r0 * 512:r0 * 512 + 2048], AF.Exp,
                        bias=BSALL[:, col:col + 1], scale=ATT_SCALE,
                        accum_out=LAALL[:, col:col + 1]))(),
                        reads=pbk(r0, 4) + [f"BS{col % 8}"], writes=SCK[b % 2] + [f"LA{col % 8}"])
            bsr = BSALL.rearrange("p (m b) -> p m b", b=4)
            dr3 = DALL.rearrange("p (m b) -> p m b", b=4)
            all_bs = [f"BS{i}" for i in range(8)]
            all_la = [f"LA{i}" for i in range(8)]
            S.emit("dve", lambda e: e.tensor_tensor(B_ALL[:, :], bsr[:, :, 0], bsr[:, :, 1], ALU.min),
                   reads=all_bs, writes=["B_ALL"])
            S.emit("dve", lambda e: e.tensor_tensor(B_ALL[:, :], B_ALL[:, :], bsr[:, :, 2], ALU.min),
                   reads=all_bs + ["B_ALL"], writes=["B_ALL"])
            S.emit("dve", lambda e: e.tensor_tensor(B_ALL[:, :], B_ALL[:, :], bsr[:, :, 3], ALU.min),
                   reads=all_bs + ["B_ALL"], writes=["B_ALL"])
            bmin = B_ALL.rearrange("p (m o) -> p m o", o=1)
            S.emit("dve", lambda e: e.tensor_sub(
                dr3, bsr, bmin.broadcast_to([128, 64, 4])),
                reads=all_bs + ["B_ALL"], writes=["DALL"])
            S.emit("act", lambda e: e.activation(FALL[:, :], DALL[:, :], AF.Exp, scale=-1.0),
                   reads=["DALL"], writes=["FALL"])
            S.emit("dve", lambda e: e.tensor_mul(PALL[:, :], FALL[:, :], LAALL[:, :]),
                   reads=["FALL"] + all_la, writes=["PALL"])
            p3 = PALL.rearrange("p (m b) -> p m b", b=4)
            S.emit("dve", lambda e: e.tensor_add(L_ALL[:, :], p3[:, :, 0], p3[:, :, 1]),
                   reads=["PALL"], writes=["L_ALL"])
            S.emit("dve", lambda e: e.tensor_add(L_ALL[:, :], L_ALL[:, :], p3[:, :, 2]),
                   reads=["PALL", "L_ALL"], writes=["L_ALL"])
            S.emit("dve", lambda e: e.tensor_add(L_ALL[:, :], L_ALL[:, :], p3[:, :, 3]),
                   reads=["PALL", "L_ALL"], writes=["L_ALL"])
            S.emit("act", lambda e: e.activation(LNL[:, :], L_ALL[:, :], AF.Ln),
                   reads=["L_ALL"], writes=["LNL"])
            S.emit("dve", lambda e: e.tensor_sub(BIASV[:, :], B_ALL[:, :], LNL[:, :]),
                   reads=["B_ALL", "LNL"], writes=["BIASV"])
            for Eb in range(8):
                def emit_sv(m):
                    sc_t = m % 4
                    S.emit("pe", [(lambda m=m, k=k, sc_t=sc_t: lambda e: e.matmul(
                        pb(4 + k), VT[:, m * 128:(m + 1) * 128],
                        SC4[sc_t][:, k * 512:(k + 1) * 512],
                        start=(m == 0), stop=(m == 63)))() for k in range(2)],
                        reads=vk + SC4K[sc_t], writes=pbk(4, 2))

                for m in range(64):
                    lv = KT[:, (m // 4) * 512 + (m % 4) * 128:(m // 4) * 512 + (m % 4) * 128 + 128]
                    r0 = (m % 2) * 2
                    S.emit("pe", [(lambda lv=lv, Eb=Eb, k=k, r0=r0: lambda e: e.matmul(
                        pb(r0 + k), lv, QT[:, Eb * 1024 + k * 512:Eb * 1024 + (k + 1) * 512],
                        start=True, stop=True))() for k in range(2)],
                        reads=kk + qk, writes=pbk(r0, 2))
                    if m > 0:
                        emit_sv(m - 1)
                    S.emit("act", (lambda r0=r0, m=m: lambda e: e.activation(
                        SC4[m % 4][:, :], PB[:, r0 * 512:r0 * 512 + 1024], AF.Exp,
                        bias=BIASV[:, m:m + 1], scale=ATT_SCALE))(),
                        reads=pbk(r0, 2) + ["BIASV"], writes=SC4K[m % 4])
                emit_sv(63)
                for k in range(2):
                    j = 2 * Eb + k
                    S.emit("dve", (lambda s=s, j=j, k=k: lambda e: e.tensor_add(
                        X[s][:, j * 512:(j + 1) * 512],
                        X[s][:, j * 512:(j + 1) * 512], pb(4 + k)))(),
                        reads=[f"pb{4 + k}", xk], writes=[xk])

        def att1(s):
            xk = f"X{s}"
            hk = f"XH{s}"
            for eo in range(24):
                sl = eo % 2
                wk = f"WBUF{sl}"
                bk = eo % 4
                S.emit("pool", (lambda eo=eo, sl=sl: lambda e: e.dma_start(
                    out=WB[sl][:, 0:2048].rearrange("p (c f) -> p c f", c=NCH),
                    in_=dr["qkv1"][eo].rearrange("c p f -> p c f")))(),
                    writes=[wk], sem=f"w{sl}", dma=True)
                S.emit("pe", [(lambda sl=sl, c=c, bk=bk: lambda e: e.matmul(
                    pb(bk), WB[sl][:, c * 128:(c + 1) * 128],
                    XH[s][:, c * 512:(c + 1) * 512], start=(c == 0), stop=(c == NCH - 1)))()
                    for c in range(NCH)],
                    reads=[wk, hk], writes=[f"pb{bk}"])
                if eo < 20:
                    S.emit("dve", (lambda eo=eo, bk=bk: lambda e: e.tensor_scalar_add(
                        SCRAP[:, :], pb(bk), CS[:, C_QB1 + eo:C_QB1 + eo + 1]))(),
                        reads=[f"pb{bk}", "CS"], writes=["HT7"])
                else:
                    g = eo - 20
                    S.emit("dve", (lambda eo=eo, g=g, bk=bk: lambda e: e.tensor_scalar_add(
                        VCH[:, g * 512:(g + 1) * 512], pb(bk), CS[:, C_QB1 + eo:C_QB1 + eo + 1]))(),
                        reads=[f"pb{bk}", "CS"], writes=VCH_KEYS)
            for eo2 in range(NCH):
                sl = eo2 % 2
                wk = f"WBUF{sl}"
                bk = eo2 % 4
                S.emit("pool", (lambda eo2=eo2, sl=sl: lambda e: e.dma_start(
                    out=WB[sl][:, 0:2048].rearrange("p (c f) -> p c f", c=NCH),
                    in_=dr["ow1"][eo2].rearrange("c p f -> p c f")))(),
                    writes=[wk], sem=f"w{sl}", dma=True)
                S.emit("pe", [(lambda sl=sl, c=c, bk=bk: lambda e: e.matmul(
                    pb(bk), WB[sl][:, c * 128:(c + 1) * 128],
                    VCH[:, (c // 4) * 512:(c // 4 + 1) * 512],
                    start=(c == 0), stop=(c == NCH - 1)))() for c in range(NCH)],
                    reads=[wk] + VCH_KEYS, writes=[f"pb{bk}"])
                S.emit("dve", (lambda s=s, eo2=eo2, bk=bk: lambda e: e.scalar_tensor_tensor(
                    X[s][:, eo2 * 512:(eo2 + 1) * 512], pb(bk),
                    CS[:, C_OB1 + eo2:C_OB1 + eo2 + 1],
                    X[s][:, eo2 * 512:(eo2 + 1) * 512], ALU.add, ALU.add))(),
                    reads=[f"pb{bk}", xk, "CS"], writes=[xk])

        for s in (0, 1):
            inload(s)
        for idx in seg:
            kind, L = SUBLAYERS[idx]
            if kind == "ffn":
                layernorm(0, 0)
                layernorm(1, 1)
                ffn_pair(L)
            else:
                for s in (0, 1):
                    layernorm(s, 0)
                    if kind == "att0":
                        att0(s)
                    else:
                        att1(s)
        for s in (0, 1):
            outstore(s)
        S.wait_only("sp", [("st", S.cnt["st"])])

        for q in QUEUES:
            for waits, _, _, _ in S.tasks[q]:
                for ps, c in waits:
                    assert c <= S.cnt[ps], (q, ps, c, S.cnt[ps])

        def runner(qname):
            def body(eng):
                for waits, fns, sem, step in S.tasks[qname]:
                    for ps, c in waits:
                        eng.wait_ge(sems[ps], c)
                    inst = None
                    for f in fns:
                        inst = f(eng)
                    if inst is not None and step:
                        inst.then_inc(sems[sem], step)
            return body

        block.tensor(runner("pe"))
        block.scalar(runner("act"))
        block.vector(runner("dve"))
        block.gpsimd(runner("pool"))
        block.sync(runner("sp"))

    return nc, names


def _pack_lhsT_in_out(w):
    nin, nout = w.shape
    a = w.reshape(nin // 128, 128, nout // 128, 128)
    return np.ascontiguousarray(a.transpose(2, 0, 1, 3))


def _pack_gu(gw, uw):
    nI = gw.shape[0] // 128
    def pk(w):
        return w.reshape(nI, 128, NCH, 128).transpose(0, 2, 3, 1)
    return np.ascontiguousarray(np.stack([pk(gw), pk(uw)], axis=1))


def _pack_down(dw):
    nI = dw.shape[1] // 128
    a = dw.reshape(4, 4, 128, nI, 128)
    return np.ascontiguousarray(a.transpose(0, 3, 4, 1, 2).reshape(4, nI, 128, 512))


def _pack_consts(i):
    cs = np.zeros((128, C_NCOLS), np.float32)
    cs[:, C_SG[0]:C_SG[0] + 64] = np.asarray(i["L0_ffn_gate_scale"], np.float32).reshape(64, 128).T
    cs[:, C_SU[0]:C_SU[0] + 64] = np.asarray(i["L0_ffn_up_scale"], np.float32).reshape(64, 128).T
    cs[:, C_SD[0]:C_SD[0] + 16] = np.asarray(i["L0_ffn_down_scale"], np.float32).reshape(16, 128).T
    cs[:, C_SG[1]:C_SG[1] + 44] = np.asarray(i["L1_ffn_s_gate"], np.float32).reshape(44, 128).T
    cs[:, C_SU[1]:C_SU[1] + 44] = np.asarray(i["L1_ffn_s_up"], np.float32).reshape(44, 128).T
    cs[:, C_SD[1]:C_SD[1] + 16] = np.asarray(i["L1_ffn_s_down"], np.float32).reshape(16, 128).T
    cs[:, C_QB1:C_QB1 + 24] = np.asarray(i["L1_attn_qkv_b"], np.float32).reshape(24, 128).T
    cs[:, C_OB1:C_OB1 + 16] = np.asarray(i["L1_attn_out_b"], np.float32).reshape(16, 128).T
    return cs


def _pack_all(i):
    p = {}
    p["cs"] = _pack_consts(i)
    cp = np.zeros((1, 256), np.float32)
    cp[0, 0:128] = 1.0
    cp[0, 128:256] = -1.0 / 2048.0
    p["cp"] = cp
    p["idb"] = np.eye(128, dtype=ml_dtypes.bfloat16)
    p["idf"] = np.eye(128, dtype=np.float32)
    p["gu0"] = _pack_gu(np.asarray(i["L0_ffn_gate_w"]), np.asarray(i["L0_ffn_up_w"]))
    p["dn0"] = _pack_down(np.asarray(i["L0_ffn_down_w"]))
    p["gu1"] = _pack_gu(np.asarray(i["L1_ffn_wq_gate"]), np.asarray(i["L1_ffn_wq_up"]))
    p["dn1"] = _pack_down(np.asarray(i["L1_ffn_wq_down"]))
    p["w3"] = np.ascontiguousarray(np.stack([
        _pack_lhsT_in_out(np.asarray(i["L0_attn_w_query"], np.float32)),
        _pack_lhsT_in_out(np.asarray(i["L0_attn_w_key"], np.float32)),
        _pack_lhsT_in_out(np.asarray(i["L0_attn_w_value"], np.float32))]))
    p["qkv1"] = _pack_lhsT_in_out(np.asarray(i["L1_attn_qkv_w"], np.float32))
    p["ow1"] = _pack_lhsT_in_out(np.asarray(i["L1_attn_out_w"], np.float32))
    return p


def kernel(**inputs):
    packed = _pack_all(inputs)
    cur = np.ascontiguousarray(np.asarray(inputs["L0_attn_x"], np.float32).reshape(8192, 2048))
    for seg in SEGMENTS:
        nc, names = build_nc(seg)
        in_maps = []
        for c in range(8):
            m = {"x": np.ascontiguousarray(cur[c * 1024:(c + 1) * 1024])}
            for n in names:
                m[n] = packed[n]
            in_maps.append(m)
        res = run_bass_kernel_spmd(nc, in_maps, core_ids=list(range(8)))
        cur = np.concatenate([r["y"] for r in res.results], axis=0)
    return cur.reshape(1, 8192, 2048).astype(np.float32)


# version 09
# speedup vs baseline: 1.1894x; 1.0871x over previous
import numpy as np
import ml_dtypes
import concourse.bass as bass
import concourse.mybir as mybir
from concourse.bass_utils import run_bass_kernel_spmd

F32 = mybir.dt.float32
BF16 = mybir.dt.bfloat16
I32 = mybir.dt.int32
AF = mybir.ActivationFunctionType
ALU = mybir.AluOpType
AX = mybir.AxisListType

NCH = 16
T = 512
NI = {0: 64, 1: 44}
HALVES = {0: [(0, 32), (32, 32)], 1: [(0, 24), (24, 20)]}
ATT_SCALE = float(1.0 / np.sqrt(128.0))
EPS = 1e-5

C_SG = {0: 0, 1: 144}
C_SU = {0: 64, 1: 188}
C_SD = {0: 128, 1: 232}
C_QB1 = 248
C_OB1 = 272
C_NCOLS = 288

SUBLAYERS = [("ffn", 0), ("att0", 0), ("ffn", 0), ("ffn", 1), ("att1", 1), ("ffn", 1)]
SEGMENTS = [[0, 1, 2, 3, 4, 5]]

SEM_NAMES = ["pe", "act", "dve", "pool", "sp", "w0", "w1", "w2", "w3", "io0", "io1", "cst", "st"]
QUEUES = ["pe", "act", "dve", "pool", "sp"]


class Sched:

    def __init__(self):
        self.tasks = {q: [] for q in QUEUES}
        self.cnt = {s: 0 for s in SEM_NAMES}
        self.floor = {}
        self.res = {}

    def emit(self, q, fns, reads=(), writes=(), sem=None, dma=False):
        if not isinstance(fns, (list, tuple)):
            fns = [fns]
        sem = sem or q
        step = 16 if dma else 1
        need = {}

        def add(w):
            if w is not None:
                need[w[0]] = max(need.get(w[0], 0), w[1])

        for k in reads:
            e = self.res.get(k)
            if e is not None:
                add(e[0])
        for k in writes:
            e = self.res.get(k)
            if e is not None:
                add(e[0])
                for r in e[1]:
                    add(r)
        waits = []
        for ps in sorted(need):
            c = need[ps]
            key = (q, ps)
            if self.floor.get(key, 0) >= c:
                continue
            waits.append((ps, c))
            self.floor[key] = c
        self.cnt[sem] += step
        mc = (sem, self.cnt[sem])
        self.tasks[q].append((waits, fns, sem, step))
        for k in reads:
            self.res.setdefault(k, [None, []])[1].append(mc)
        for k in writes:
            self.res[k] = [mc, []]

    def wait_only(self, q, sems_counts):
        self.tasks[q].append((list(sems_counts), [], None, 0))


def ht_keys(c0, c1):
    return [f"HT{k}" for k in range(c0 // 4096, (c1 - 1) // 4096 + 1)]


def build_nc(seg):
    kinds = {SUBLAYERS[i][0] for i in seg}
    layers = {SUBLAYERS[i][1] for i in seg if SUBLAYERS[i][0] == "ffn"}

    nc = bass.Bass("TRN2", target_bir_lowering=False)
    x_in = nc.dram_tensor("x", [1024, 2048], F32, kind="ExternalInput").ap()
    y_out = nc.dram_tensor("y", [1024, 2048], F32, kind="ExternalOutput").ap()
    cs_in = nc.dram_tensor("cs", [128, C_NCOLS], F32, kind="ExternalInput").ap()
    cp_in = nc.dram_tensor("cp", [1, 256], F32, kind="ExternalInput").ap()
    idb_in = nc.dram_tensor("idb", [128, 128], BF16, kind="ExternalInput").ap()
    idf_in = nc.dram_tensor("idf", [128, 128], F32, kind="ExternalInput").ap()
    names = ["cs", "cp", "idb", "idf"]
    dr = {}
    for L in layers:
        dr[f"gu{L}"] = nc.dram_tensor(f"gu{L}", [NI[L], 2, NCH, 128, 128], I32, kind="ExternalInput").ap()
        dr[f"dn{L}"] = nc.dram_tensor(f"dn{L}", [4, NI[L], 128, 512], I32, kind="ExternalInput").ap()
        names += [f"gu{L}", f"dn{L}"]
    if "att0" in kinds:
        dr["w3"] = nc.dram_tensor("w3", [3, NCH, NCH, 128, 128], F32, kind="ExternalInput").ap()
        names += ["w3"]
    if "att1" in kinds:
        dr["qkv1"] = nc.dram_tensor("qkv1", [24, NCH, 128, 128], F32, kind="ExternalInput").ap()
        dr["ow1"] = nc.dram_tensor("ow1", [NCH, NCH, 128, 128], F32, kind="ExternalInput").ap()
        names += ["qkv1", "ow1"]

    S = Sched()

    from contextlib import ExitStack
    with ExitStack() as st_:
        def sb(name, shape, dt):
            return st_.enter_context(nc.sbuf_tensor(name, shape, dt))

        X = [sb("X0", [128, NCH * T], F32), sb("X1", [128, NCH * T], F32)]
        XH = [sb("XHa", [128, NCH * T], BF16), sb("XHb", [128, NCH * T], BF16)]
        HT = sb("HT", [128, 64 * T], BF16)
        WB = [sb("WB0", [128, 4096], BF16), sb("WB1", [128, 4096], BF16)]
        WH = [WB[i // 2][:, (i % 2) * 2048:(i % 2 + 1) * 2048] for i in range(4)]
        SG = [sb("SGa", [128, T], BF16), sb("SGb", [128, T], BF16)]
        VFS = SG
        T1 = [sb(f"T1{i}", [128, T], F32) for i in range(4)]
        SQ = [sb(f"SQ{i}", [128, T], BF16) for i in range(4)]
        CS = sb("CS", [128, C_NCOLS], F32)
        CP = sb("CP", [1, 256], F32)
        IDB = sb("IDB", [128, 128], BF16)
        IDF = sb("IDF", [128, 128], F32)
        ONESB = sb("ONESB", [128, 1], BF16)
        EPSC = sb("EPSC", [128, 1], F32)
        HS = sb("HS", [128, 32], F32)
        SXR = sb("SXR", [1, 1024], F32)
        ROWA = sb("ROWA", [1, T], F32)
        ROWB = sb("ROWB", [1, T], F32)
        ROWC = sb("ROWC", [1, T], F32)
        SXRv = [SXR, SXR]
        ROWAv = [ROWA, ROWA]
        ROWBv = [ROWB, ROWB]
        ROWCv = [ROWC, ROWC]
        NM = sb("NM", [128, 1], F32)
        BSALL = sb("BSALL", [128, 256], F32)
        LAALL = sb("LAALL", [128, 256], F32)
        DALL = sb("DALL", [128, 256], F32)
        FALL = DALL
        PALL = DALL
        L_ALL = sb("L_ALL", [128, 64], F32)
        B_ALL = sb("B_ALL", [128, 64], F32)
        LNL = sb("LNL", [128, 64], F32)
        BIASV = sb("BIASV", [128, 64], F32)
        PB = st_.enter_context(nc.psum_tensor("PB", [128, 4096], F32))
        sems = {n: st_.enter_context(nc.semaphore(n)) for n in SEM_NAMES}
        block = st_.enter_context(nc.Block())

        def pb(b, n=1):
            return PB[:, b * 512:(b + n) * 512]

        def pbk(b, n=1):
            return [f"pb{i}" for i in range(b, b + n)]

        HTF = HT[:, :].bitcast(F32)
        QT = HT[:, 0:8192]
        KT = HT[:, 8192:16384]
        VT = HT[:, 16384:24576]
        VCH = HT[:, 24576:26624]
        VCH_KEYS = ht_keys(24576, 26624)
        SC = [HT[:, 26624:28672], HT[:, 28672:30720]]
        SCK = [["HT6"], ["HT7"]]
        SC4 = [HT[:, 26624 + 1024 * i:26624 + 1024 * (i + 1)] for i in range(4)]
        SC4K = [["HT6"], ["HT6"], ["HT7"], ["HT7"]]
        SCRAP = HT[:, 30720:31232]

        S.emit("sp", lambda e: e.dma_start(out=CS[:, :], in_=cs_in), writes=["CS"], sem="cst", dma=True)
        S.emit("sp", lambda e: e.dma_start(out=CP[:, :], in_=cp_in), writes=["CP"], sem="cst", dma=True)
        S.emit("sp", lambda e: e.dma_start(out=IDB[:, :], in_=idb_in), writes=["IDB"], sem="cst", dma=True)
        S.emit("sp", lambda e: e.dma_start(out=IDF[:, :], in_=idf_in), writes=["IDF"], sem="cst", dma=True)
        S.emit("dve", lambda e: e.memset(ONESB[:, :], 1.0), writes=["ONESB"])
        S.emit("dve", lambda e: e.memset(EPSC[:, :], EPS), writes=["EPSC"])
        for L in layers:
            c0, h0 = C_SD[L], 16 * L
            S.emit("dve", (lambda c0=c0, h0=h0: lambda e: e.tensor_scalar_mul(
                HS[:, h0:h0 + 16], CS[:, c0:c0 + 16], 0.5))(),
                reads=["CS"], writes=["HS"])

        def inload(s):
            for g2 in range(2):
                slot = g2 % 2
                TM = HTF[:, slot * 4096:(slot + 1) * 4096]
                tk = [f"HT{2 * slot}", f"HT{2 * slot + 1}"]
                r0 = s * 512 + g2 * 256
                S.emit("sp", (lambda TM=TM, r0=r0: lambda e: e.dma_start(
                    out=TM.rearrange("p (g f) -> p g f", g=2),
                    in_=x_in[r0:r0 + 256, :].rearrange("(g p) f -> p g f", g=2)))(),
                    writes=tk, sem=f"io{slot}", dma=True)
                for gg in range(2):
                    g = g2 * 2 + gg
                    for j in range(NCH):
                        bk = j % 4
                        S.emit("pe", (lambda TM=TM, gg=gg, j=j, bk=bk: lambda e: e.transpose(
                            PB[:, bk * 512:bk * 512 + 128],
                            TM[:, gg * 2048 + j * 128:gg * 2048 + (j + 1) * 128], IDF[:, :]))(),
                            reads=tk + ["IDF"], writes=[f"pb{bk}"])
                        S.emit("dve", (lambda s=s, j=j, g=g, bk=bk: lambda e: e.tensor_copy(
                            X[s][:, j * 512 + g * 128:j * 512 + g * 128 + 128],
                            PB[:, bk * 512:bk * 512 + 128]))(),
                            reads=[f"pb{bk}"], writes=[f"X{s}"])

        def outstore(s):
            for g in range(4):
                slot = g % 2
                TM = HTF[:, slot * 2048:(slot + 1) * 2048]
                tk = f"HT{slot}"
                r0 = s * 512 + g * 128
                for j in range(NCH):
                    bk = j % 4
                    S.emit("pe", (lambda s=s, j=j, g=g, bk=bk: lambda e: e.transpose(
                        PB[:, bk * 512:bk * 512 + 128],
                        X[s][:, j * 512 + g * 128:j * 512 + g * 128 + 128], IDF[:, :]))(),
                        reads=[f"X{s}", "IDF"], writes=[f"pb{bk}"])
                    S.emit("dve", (lambda TM=TM, j=j, bk=bk: lambda e: e.tensor_copy(
                        TM[:, j * 128:(j + 1) * 128], PB[:, bk * 512:bk * 512 + 128]))(),
                        reads=[f"pb{bk}"], writes=[tk])
                S.emit("sp", (lambda TM=TM, r0=r0: lambda e: e.dma_start(
                    out=y_out[r0:r0 + 128, :], in_=TM))(),
                    reads=[tk], sem="st", dma=True)

        LNB = [(6, 7, 4, 5), (2, 3, 0, 1)]

        def ln_sums(s, v):
            hk = f"XH{s}"
            XHs = XH[s]
            bsx, bsq, _, _ = LNB[v]
            for p4 in range(4):
                S.emit("dve", (lambda s=s, XHs=XHs, p4=p4: lambda e: e.tensor_copy(
                    XHs[:, p4 * 2048:(p4 + 1) * 2048], X[s][:, p4 * 2048:(p4 + 1) * 2048]))(),
                    reads=[f"X{s}"], writes=[f"{hk}p{p4}"])
                S.emit("pe", [(lambda j=j, XHs=XHs, bsx=bsx: lambda e: e.matmul(
                    PB[0:1, bsx * 512:bsx * 512 + 512], ONESB[:, :], XHs[:, j * 512:(j + 1) * 512],
                    start=(j == 0), stop=(j == NCH - 1)))() for j in range(p4 * 4, p4 * 4 + 4)],
                    reads=[f"{hk}p{p4}", "ONESB"], writes=[f"pb{bsx}"])
            for j in range(NCH):
                sq = (j % 2) + 2 * v
                S.emit("act", (lambda j=j, sq=sq, XHs=XHs: lambda e: e.activation(
                    SQ[sq][:, :], XHs[:, j * 512:(j + 1) * 512], AF.Square))(),
                    reads=[f"{hk}p{j // 4}"], writes=[f"SQ{sq}"])
                S.emit("pe", (lambda j=j, sq=sq, bsq=bsq: lambda e: e.matmul(
                    PB[0:1, bsq * 512:bsq * 512 + 512], ONESB[:, :], SQ[sq][:, :],
                    start=(j == 0), stop=(j == NCH - 1)))(),
                    reads=[f"SQ{sq}", "ONESB"], writes=[f"pb{bsq}"])

        def ln_rows(s, v):
            bsx, bsq, bm, br = LNB[v]
            SX, RA, RB, RC = SXRv[v], ROWAv[v], ROWBv[v], ROWCv[v]
            kx, ka, kb, kc = "SXR", "ROWA", "ROWB", "ROWC"
            S.emit("dve", (lambda: lambda e: e.tensor_copy(SX[:, 0:512], PB[0:1, bsx * 512:bsx * 512 + 512]))(),
                   reads=[f"pb{bsx}"], writes=[kx])
            S.emit("dve", (lambda: lambda e: e.tensor_copy(SX[:, 512:1024], PB[0:1, bsq * 512:bsq * 512 + 512]))(),
                   reads=[f"pb{bsq}"], writes=[kx])
            S.emit("dve", (lambda: lambda e: e.tensor_scalar_mul(RA[:, :], SX[:, 0:512], 1.0 / 2048.0))(),
                   reads=[kx], writes=[ka])
            S.emit("dve", (lambda: lambda e: e.tensor_mul(RB[:, :], RA[:, :], RA[:, :]))(),
                   reads=[ka], writes=[kb])
            S.emit("dve", (lambda: lambda e: e.scalar_tensor_tensor(
                RC[:, :], SX[:, 512:1024], 1.0 / 2048.0, RB[:, :], ALU.mult, ALU.subtract))(),
                reads=[kx, kb], writes=[kc])
            S.emit("act", (lambda: lambda e: e.activation(RB[:, :], RC[:, :], AF.Ln, bias=EPSC[0:1, 0:1]))(),
                   reads=[kc, "EPSC"], writes=[kb])
            S.emit("act", (lambda: lambda e: e.activation(RA[:, :], RB[:, :], AF.Exp, scale=-0.5))(),
                   reads=[kb], writes=[ka])
            S.emit("pe", (lambda: lambda e: e.matmul(pb(bm), CP[0:1, 128:256], SX[0:1, 0:512],
                                                     start=True, stop=True))(),
                   reads=[kx, "CP"], writes=[f"pb{bm}"])
            S.emit("pe", (lambda: lambda e: e.matmul(pb(br), CP[0:1, 0:128], RA[0:1, :],
                                                     start=True, stop=True))(),
                   reads=[ka, "CP"], writes=[f"pb{br}"])

        def ln_apply(s, v):
            _, _, bm, br = LNB[v]
            hk = f"XH{s}"
            for j in range(NCH):
                t = (j % 2) + 2 * v
                S.emit("dve", (lambda s=s, j=j, t=t, bm=bm: lambda e: e.tensor_add(
                    T1[t][:, :], X[s][:, j * 512:(j + 1) * 512], pb(bm)))(),
                    reads=[f"X{s}", f"pb{bm}"], writes=[f"T1{t}"])
                S.emit("dve", (lambda s=s, j=j, t=t, br=br: lambda e: e.tensor_mul(
                    XH[s][:, j * 512:(j + 1) * 512], T1[t][:, :], pb(br)))(),
                    reads=[f"T1{t}", f"pb{br}"], writes=[f"{hk}p{j // 4}"])

        def layernorm(s, v=0):
            ln_sums(s, v)
            ln_rows(s, v)
            ln_apply(s, v)

        def ln_pair():
            ln_sums(0, 0)
            ln_sums(1, 1)
            ln_rows(0, 0)
            ln_rows(1, 1)
            ln_apply(0, 0)
            ln_apply(1, 1)

        def hkey(s, k):
            c0 = s * 16384 + k * 512
            return ht_keys(c0, c0 + 512)

        def ffn_pair(L):
            nI = NI[L]
            gu, dn = dr[f"gu{L}"], dr[f"dn{L}"]
            sgc, suc = C_SG[L], C_SU[L]
            pend = None
            dpend = None
            for (h0, hlen) in HALVES[L]:
                for k in range(hlen):
                    i = h0 + k
                    sl = k % 2
                    wk = f"WBUF{sl}"
                    bk = (k % 2) * 4
                    S.emit("pool", (lambda i=i, sl=sl: lambda e: e.dma_start(
                        out=WB[sl][:, :].rearrange("p (w c f) -> p w c f", w=2, c=NCH),
                        in_=gu[i].rearrange("w c p f -> p w c f")))(),
                        writes=[f"WH{2*sl}", f"WH{2*sl+1}"], sem=f"w{2*sl}", dma=True)
                    S.emit("dve", (lambda sl=sl: lambda e: e.tensor_scalar_add(
                        WB[sl][:, :], WB[sl][:, :], -128.0))(),
                        reads=[f"WH{2*sl}", f"WH{2*sl+1}"], writes=[f"WH{2*sl}", f"WH{2*sl+1}"])
                    if dpend is not None:
                        if k == 0:
                            dpend(only=0)
                        else:
                            dpend(only=1)
                            dpend = None
                    S.emit("pe", [(lambda sl=sl, c=c, s=s, bk=bk: lambda e: e.matmul(
                        pb(bk + 2 * s), WB[sl][:, c * 128:(c + 1) * 128],
                        XH[s][:, c * 512:(c + 1) * 512], start=(c == 0), stop=(c == NCH - 1)))()
                        for c in range(NCH) for s in range(2)],
                        reads=[f"WH{2*sl}", f"WH{2*sl+1}"] + [f"XH0p{q}" for q in range(4)] + [f"XH1p{q}" for q in range(4)], writes=[f"pb{bk}", f"pb{bk + 2}"])
                    S.emit("pe", [(lambda sl=sl, c=c, s=s, bk=bk: lambda e: e.matmul(
                        pb(bk + 1 + 2 * s), WB[sl][:, (NCH + c) * 128:(NCH + c + 1) * 128],
                        XH[s][:, c * 512:(c + 1) * 512], start=(c == 0), stop=(c == NCH - 1)))()
                        for c in range(NCH) for s in range(2)],
                        reads=[f"WH{2*sl}", f"WH{2*sl+1}"] + [f"XH0p{q}" for q in range(4)] + [f"XH1p{q}" for q in range(4)], writes=[f"pb{bk + 1}", f"pb{bk + 3}"])
                    def guevac(i=i, k=k, bk=bk):
                        for s in range(2):
                            S.emit("act", (lambda i=i, s=s, bk=bk: lambda e: e.activation(
                                SG[s][:, :], pb(bk + 2 * s), AF.Silu, scale=CS[:, sgc + i:sgc + i + 1]))(),
                                reads=[f"pb{bk + 2 * s}", "CS"], writes=[f"SG{s}"])
                            S.emit("dve", (lambda i=i, s=s, k=k, bk=bk: lambda e: e.scalar_tensor_tensor(
                                HT[:, s * 16384 + k * 512:s * 16384 + (k + 1) * 512],
                                pb(bk + 1 + 2 * s), CS[:, suc + i:suc + i + 1],
                                SG[s][:, :], ALU.mult, ALU.mult))(),
                                reads=[f"pb{bk + 1 + 2 * s}", f"SG{s}", "CS"],
                                writes=hkey(s, k))
                    if pend is not None:
                        pend()
                    pend = guevac
                for q in range(4):
                    cbs = [(h0 + a, min(4, hlen - a)) for a in range(0, hlen, 4)]
                    first_batch = True
                    bi = -1
                    for (c0, bs) in cbs:
                        bi += 1
                        sl = bi % 4
                        wk = f"WH{sl}"
                        S.emit("pool", (lambda q=q, c0=c0, bs=bs, sl=sl: lambda e: e.dma_start(
                            out=WH[sl][:, 0:bs * 512].rearrange("p (c f) -> p c f", c=bs),
                            in_=dn[q, c0:c0 + bs].rearrange("c p f -> p c f")))(),
                            writes=[wk], sem=f"w{sl}", dma=True)
                        S.emit("dve", (lambda bs=bs, sl=sl: lambda e: e.tensor_scalar_add(
                            WH[sl][:, 0:bs * 512], WH[sl][:, 0:bs * 512], -128.0))(),
                            reads=[wk], writes=[wk])
                        if first_batch:
                            if pend is not None:
                                pend()
                                pend = None
                            if dpend is not None:
                                dpend()
                                dpend = None
                        first_batch = False
                        for c4 in range(bs):
                            c = c0 + c4
                            kin = c - h0
                            S.emit("pe", [(lambda sl=sl, c4=c4, c=c, kin=kin, s=s, o=o, h0=h0, hlen=hlen: lambda e: e.matmul(
                                pb(s * 4 + o), WH[sl][:, c4 * 512 + o * 128:c4 * 512 + (o + 1) * 128],
                                HT[:, s * 16384 + kin * 512:s * 16384 + (kin + 1) * 512],
                                start=(c == h0), stop=(c == h0 + hlen - 1)))()
                                for o in range(4) for s in range(2)],
                                reads=[wk] + hkey(0, kin) + hkey(1, kin),
                                writes=[f"pb{b}" for b in range(8)])
                    def devac(q=q, only=None):
                        for s in ((0, 1) if only is None else (only,)):
                            for o in range(4):
                                ho = 4 * q + o
                                S.emit("dve", (lambda s=s, L=L, ho=ho, o=o: lambda e: e.scalar_tensor_tensor(
                                    X[s][:, ho * 512:(ho + 1) * 512], pb(s * 4 + o),
                                    HS[:, 16 * L + ho:16 * L + ho + 1],
                                    X[s][:, ho * 512:(ho + 1) * 512], ALU.mult, ALU.add))(),
                                    reads=[f"pb{s * 4 + o}", f"X{s}", "HS"], writes=[f"X{s}"])
                    dpend = devac
            if dpend is not None:
                dpend()

        def att0(s):
            xk = f"X{s}"
            hk = f"XH{s}"
            for w in range(3):
                for eo in range(NCH):
                    sl = (w * NCH + eo) % 4
                    wk = f"WH{sl}"
                    bk = (eo % 4) if w < 2 else (eo % 2)
                    S.emit("pool", (lambda w=w, eo=eo, sl=sl: lambda e: e.dma_start(
                        out=WH[sl].rearrange("p (c f) -> p c f", c=NCH),
                        in_=dr["w3"][w, eo].rearrange("c p f -> p c f")))(),
                        writes=[wk], sem=f"w{sl}", dma=True)
                    S.emit("pe", [(lambda sl=sl, c=c, bk=bk: lambda e: e.matmul(
                        pb(bk), WH[sl][:, c * 128:(c + 1) * 128],
                        XH[s][:, c * 512:(c + 1) * 512], start=(c == 0), stop=(c == NCH - 1)))()
                        for c in range(NCH)],
                        reads=[wk] + [f"{hk}p{q}" for q in range(4)], writes=[f"pb{bk}"])
                    if w == 0:
                        S.emit("act", (lambda eo=eo, bk=bk: lambda e: e.mul(
                            QT[:, eo * 512:(eo + 1) * 512], pb(bk), ATT_SCALE))(),
                            reads=[f"pb{bk}"], writes=ht_keys(eo * 512, (eo + 1) * 512))
                    elif w == 1:
                        S.emit("dve", (lambda eo=eo, bk=bk: lambda e: e.tensor_copy(
                            KT[:, eo * 512:(eo + 1) * 512], pb(bk)))(),
                            reads=[f"pb{bk}"], writes=ht_keys(8192 + eo * 512, 8192 + (eo + 1) * 512))
                    else:
                        vs = eo % 2
                        S.emit("dve", (lambda eo=eo, bk=bk, vs=vs: lambda e: e.tensor_copy(
                            VFS[vs][:, :], pb(bk)))(),
                            reads=[f"pb{bk}"], writes=[f"SG{vs}"])
                        for t4 in range(4):
                            m = eo * 4 + t4
                            tb = 2 + t4 % 2
                            S.emit("pe", (lambda vs=vs, t4=t4, tb=tb: lambda e: e.transpose(
                                PB[:, tb * 512:tb * 512 + 256].bitcast(BF16)[:, 0:128],
                                VFS[vs][:, t4 * 128:(t4 + 1) * 128], IDB[:, :]))(),
                                reads=[f"SG{vs}", "IDB"], writes=[f"pb{tb}"])
                            S.emit("dve", (lambda m=m, tb=tb: lambda e: e.tensor_copy(
                                VT[:, m * 128:(m + 1) * 128],
                                PB[:, tb * 512:tb * 512 + 256].bitcast(BF16)[:, 0:128]))(),
                                reads=[f"pb{tb}"],
                                writes=ht_keys(16384 + m * 128, 16384 + (m + 1) * 128))
            qk = ht_keys(0, 8192)
            kk = ht_keys(8192, 16384)
            vk = ht_keys(16384, 24576)
            for m in range(64):
                lv = KT[:, (m // 4) * 512 + (m % 4) * 128:(m // 4) * 512 + (m % 4) * 128 + 128]
                kkm = ht_keys(8192 + (m // 4) * 512, 8192 + (m // 4) * 512 + 512)
                for b in range(4):
                    r0 = (b % 2) * 4
                    col = 4 * m + b
                    S.emit("pe", [(lambda lv=lv, b=b, k=k, r0=r0: lambda e: e.matmul(
                        pb(r0 + k), lv, QT[:, b * 2048 + k * 512:b * 2048 + (k + 1) * 512],
                        start=True, stop=True))() for k in range(4)],
                        reads=kkm + ht_keys(b * 2048, (b + 1) * 2048), writes=pbk(r0, 4))
                    S.emit("dve", (lambda r0=r0, col=col: lambda e: e.reduce_max(
                        BSALL[:, col:col + 1], PB[:, r0 * 512:r0 * 512 + 2048],
                        axis=AX.X, negate=True))(),
                        reads=pbk(r0, 4), writes=[f"BS{col % 8}"])
                    S.emit("act", (lambda r0=r0, col=col, b=b: lambda e: e.activation(
                        SC[b % 2][:, :], PB[:, r0 * 512:r0 * 512 + 2048], AF.Exp,
                        bias=BSALL[:, col:col + 1],
                        accum_out=LAALL[:, col:col + 1]))(),
                        reads=pbk(r0, 4) + [f"BS{col % 8}"], writes=SCK[b % 2] + [f"LA{col % 8}"])
            bsr = BSALL.rearrange("p (m b) -> p m b", b=4)
            dr3 = DALL.rearrange("p (m b) -> p m b", b=4)
            all_bs = [f"BS{i}" for i in range(8)]
            all_la = [f"LA{i}" for i in range(8)]
            S.emit("dve", lambda e: e.tensor_tensor(B_ALL[:, :], bsr[:, :, 0], bsr[:, :, 1], ALU.min),
                   reads=all_bs, writes=["B_ALL"])
            S.emit("dve", lambda e: e.tensor_tensor(B_ALL[:, :], B_ALL[:, :], bsr[:, :, 2], ALU.min),
                   reads=all_bs + ["B_ALL"], writes=["B_ALL"])
            S.emit("dve", lambda e: e.tensor_tensor(B_ALL[:, :], B_ALL[:, :], bsr[:, :, 3], ALU.min),
                   reads=all_bs + ["B_ALL"], writes=["B_ALL"])
            bmin = B_ALL.rearrange("p (m o) -> p m o", o=1)
            S.emit("dve", lambda e: e.tensor_sub(
                dr3, bsr, bmin.broadcast_to([128, 64, 4])),
                reads=all_bs + ["B_ALL"], writes=["DALL"])
            S.emit("act", lambda e: e.activation(FALL[:, :], DALL[:, :], AF.Exp, scale=-1.0),
                   reads=["DALL"], writes=["FALL"])
            S.emit("dve", lambda e: e.tensor_mul(PALL[:, :], FALL[:, :], LAALL[:, :]),
                   reads=["FALL"] + all_la, writes=["PALL"])
            p3 = PALL.rearrange("p (m b) -> p m b", b=4)
            S.emit("dve", lambda e: e.tensor_add(L_ALL[:, :], p3[:, :, 0], p3[:, :, 1]),
                   reads=["PALL"], writes=["L_ALL"])
            S.emit("dve", lambda e: e.tensor_add(L_ALL[:, :], L_ALL[:, :], p3[:, :, 2]),
                   reads=["PALL", "L_ALL"], writes=["L_ALL"])
            S.emit("dve", lambda e: e.tensor_add(L_ALL[:, :], L_ALL[:, :], p3[:, :, 3]),
                   reads=["PALL", "L_ALL"], writes=["L_ALL"])
            S.emit("act", lambda e: e.activation(LNL[:, :], L_ALL[:, :], AF.Ln),
                   reads=["L_ALL"], writes=["LNL"])
            S.emit("dve", lambda e: e.tensor_sub(BIASV[:, :], B_ALL[:, :], LNL[:, :]),
                   reads=["B_ALL", "LNL"], writes=["BIASV"])
            for Eb in range(8):
                def emit_sv(m):
                    sc_t = m % 4
                    S.emit("pe", [(lambda m=m, k=k, sc_t=sc_t: lambda e: e.matmul(
                        pb(4 + k), VT[:, m * 128:(m + 1) * 128],
                        SC4[sc_t][:, k * 512:(k + 1) * 512],
                        start=(m == 0), stop=(m == 63)))() for k in range(2)],
                        reads=vk + SC4K[sc_t], writes=pbk(4, 2))

                for m in range(64):
                    lv = KT[:, (m // 4) * 512 + (m % 4) * 128:(m // 4) * 512 + (m % 4) * 128 + 128]
                    r0 = (m % 2) * 2
                    S.emit("pe", [(lambda lv=lv, Eb=Eb, k=k, r0=r0: lambda e: e.matmul(
                        pb(r0 + k), lv, QT[:, Eb * 1024 + k * 512:Eb * 1024 + (k + 1) * 512],
                        start=True, stop=True))() for k in range(2)],
                        reads=ht_keys(8192 + (m // 4) * 512, 8192 + (m // 4) * 512 + 512)
                        + ht_keys(Eb * 1024, (Eb + 1) * 1024), writes=pbk(r0, 2))
                    if m > 0:
                        emit_sv(m - 1)
                    S.emit("act", (lambda r0=r0, m=m: lambda e: e.activation(
                        SC4[m % 4][:, :], PB[:, r0 * 512:r0 * 512 + 1024], AF.Exp,
                        bias=BIASV[:, m:m + 1]))(),
                        reads=pbk(r0, 2) + ["BIASV"], writes=SC4K[m % 4])
                emit_sv(63)
                for k in range(2):
                    j = 2 * Eb + k
                    S.emit("dve", (lambda s=s, j=j, k=k: lambda e: e.tensor_add(
                        X[s][:, j * 512:(j + 1) * 512],
                        X[s][:, j * 512:(j + 1) * 512], pb(4 + k)))(),
                        reads=[f"pb{4 + k}", xk], writes=[xk])

        def att1_pair():
            VCHs = [HT[:, 24576:26624], HT[:, 26624:28672]]
            VCHK = [ht_keys(24576, 26624), ht_keys(26624, 28672)]
            xhk = [[f"XH0p{q}" for q in range(4)], [f"XH1p{q}" for q in range(4)]]
            for eo in range(24):
                sl = eo % 4
                wk = f"WH{sl}"
                bk = eo % 4
                S.emit("pool", (lambda eo=eo, sl=sl: lambda e: e.dma_start(
                    out=WH[sl].rearrange("p (c f) -> p c f", c=NCH),
                    in_=dr["qkv1"][eo].rearrange("c p f -> p c f")))(),
                    writes=[wk], sem=f"w{sl}", dma=True)
                S.emit("pe", [(lambda sl=sl, c=c, s=s, bk=bk: lambda e: e.matmul(
                    pb((bk % 4) + 4 * s),
                    WH[sl][:, c * 128:(c + 1) * 128],
                    XH[s][:, c * 512:(c + 1) * 512], start=(c == 0), stop=(c == NCH - 1)))()
                    for c in range(NCH) for s in range(2)],
                    reads=[wk] + xhk[0] + xhk[1], writes=[f"pb{bk}", f"pb{bk + 4}"])
                for s in range(2):
                    b2 = bk + 4 * s
                    if eo < 20:
                        S.emit("dve", (lambda eo=eo, b2=b2: lambda e: e.tensor_scalar_add(
                            SCRAP[:, :], pb(b2), CS[:, C_QB1 + eo:C_QB1 + eo + 1]))(),
                            reads=[f"pb{b2}", "CS"], writes=["HT7"])
                    else:
                        g = eo - 20
                        S.emit("dve", (lambda eo=eo, g=g, s=s, b2=b2: lambda e: e.tensor_scalar_add(
                            VCHs[s][:, g * 512:(g + 1) * 512], pb(b2),
                            CS[:, C_QB1 + eo:C_QB1 + eo + 1]))(),
                            reads=[f"pb{b2}", "CS"], writes=VCHK[s])
            for eo2 in range(NCH):
                sl = eo2 % 4
                wk = f"WH{sl}"
                bk = eo2 % 4
                S.emit("pool", (lambda eo2=eo2, sl=sl: lambda e: e.dma_start(
                    out=WH[sl].rearrange("p (c f) -> p c f", c=NCH),
                    in_=dr["ow1"][eo2].rearrange("c p f -> p c f")))(),
                    writes=[wk], sem=f"w{sl}", dma=True)
                S.emit("pe", [(lambda sl=sl, c=c, s=s, bk=bk: lambda e: e.matmul(
                    pb(bk + 4 * s), WH[sl][:, c * 128:(c + 1) * 128],
                    VCHs[s][:, (c // 4) * 512:(c // 4 + 1) * 512],
                    start=(c == 0), stop=(c == NCH - 1)))() for c in range(NCH) for s in range(2)],
                    reads=[wk] + VCHK[0] + VCHK[1], writes=[f"pb{bk}", f"pb{bk + 4}"])
                for s in range(2):
                    b2 = bk + 4 * s
                    S.emit("dve", (lambda s=s, eo2=eo2, b2=b2: lambda e: e.scalar_tensor_tensor(
                        X[s][:, eo2 * 512:(eo2 + 1) * 512], pb(b2),
                        CS[:, C_OB1 + eo2:C_OB1 + eo2 + 1],
                        X[s][:, eo2 * 512:(eo2 + 1) * 512], ALU.add, ALU.add))(),
                        reads=[f"pb{b2}", f"X{s}", "CS"], writes=[f"X{s}"])

        for s in (0, 1):
            inload(s)
        for idx in seg:
            kind, L = SUBLAYERS[idx]
            if kind == "ffn":
                layernorm(0, 0)
                layernorm(1, 1)
                ffn_pair(L)
            elif kind == "att0":
                for s in (0, 1):
                    layernorm(s, 0)
                    att0(s)
            else:
                layernorm(0, 0)
                layernorm(1, 1)
                att1_pair()
        for s in (0, 1):
            outstore(s)
        S.wait_only("sp", [("st", S.cnt["st"])])

        for q in QUEUES:
            for waits, _, _, _ in S.tasks[q]:
                for ps, c in waits:
                    assert c <= S.cnt[ps], (q, ps, c, S.cnt[ps])

        def runner(qname):
            def body(eng):
                for waits, fns, sem, step in S.tasks[qname]:
                    for ps, c in waits:
                        eng.wait_ge(sems[ps], c)
                    inst = None
                    for f in fns:
                        inst = f(eng)
                    if inst is not None and step:
                        inst.then_inc(sems[sem], step)
            return body

        block.tensor(runner("pe"))
        block.scalar(runner("act"))
        block.vector(runner("dve"))
        block.gpsimd(runner("pool"))
        block.sync(runner("sp"))

    return nc, names


def _pack_lhsT_in_out(w):
    nin, nout = w.shape
    a = w.reshape(nin // 128, 128, nout // 128, 128)
    return np.ascontiguousarray(a.transpose(2, 0, 1, 3))


def _pack_gu(gw, uw):
    nI = gw.shape[0] // 128
    def pk(w):
        return w.reshape(nI, 128, NCH, 128).transpose(0, 2, 3, 1)
    return np.ascontiguousarray(np.stack([pk(gw), pk(uw)], axis=1))


def _pack_down(dw):
    nI = dw.shape[1] // 128
    a = dw.reshape(4, 4, 128, nI, 128)
    return np.ascontiguousarray(a.transpose(0, 3, 4, 1, 2).reshape(4, nI, 128, 512))


def _pack_consts(i):
    cs = np.zeros((128, C_NCOLS), np.float32)
    cs[:, C_SG[0]:C_SG[0] + 64] = np.asarray(i["L0_ffn_gate_scale"], np.float32).reshape(64, 128).T
    cs[:, C_SU[0]:C_SU[0] + 64] = np.asarray(i["L0_ffn_up_scale"], np.float32).reshape(64, 128).T
    cs[:, C_SD[0]:C_SD[0] + 16] = np.asarray(i["L0_ffn_down_scale"], np.float32).reshape(16, 128).T
    cs[:, C_SG[1]:C_SG[1] + 44] = np.asarray(i["L1_ffn_s_gate"], np.float32).reshape(44, 128).T
    cs[:, C_SU[1]:C_SU[1] + 44] = np.asarray(i["L1_ffn_s_up"], np.float32).reshape(44, 128).T
    cs[:, C_SD[1]:C_SD[1] + 16] = np.asarray(i["L1_ffn_s_down"], np.float32).reshape(16, 128).T
    cs[:, C_QB1:C_QB1 + 24] = np.asarray(i["L1_attn_qkv_b"], np.float32).reshape(24, 128).T
    cs[:, C_OB1:C_OB1 + 16] = np.asarray(i["L1_attn_out_b"], np.float32).reshape(16, 128).T
    return cs


def _pack_all(i):
    p = {}
    p["cs"] = _pack_consts(i)
    cp = np.zeros((1, 256), np.float32)
    cp[0, 0:128] = 1.0
    cp[0, 128:256] = -1.0 / 2048.0
    p["cp"] = cp
    p["idb"] = np.eye(128, dtype=ml_dtypes.bfloat16)
    p["idf"] = np.eye(128, dtype=np.float32)
    p["gu0"] = _pack_gu(np.asarray(i["L0_ffn_gate_w"]), np.asarray(i["L0_ffn_up_w"]))
    p["dn0"] = _pack_down(np.asarray(i["L0_ffn_down_w"]))
    p["gu1"] = _pack_gu(np.asarray(i["L1_ffn_wq_gate"]), np.asarray(i["L1_ffn_wq_up"]))
    p["dn1"] = _pack_down(np.asarray(i["L1_ffn_wq_down"]))
    p["w3"] = np.ascontiguousarray(np.stack([
        _pack_lhsT_in_out(np.asarray(i["L0_attn_w_query"], np.float32)),
        _pack_lhsT_in_out(np.asarray(i["L0_attn_w_key"], np.float32)),
        _pack_lhsT_in_out(np.asarray(i["L0_attn_w_value"], np.float32))]))
    p["qkv1"] = _pack_lhsT_in_out(np.asarray(i["L1_attn_qkv_w"], np.float32))
    p["ow1"] = _pack_lhsT_in_out(np.asarray(i["L1_attn_out_w"], np.float32))
    return p


def kernel(**inputs):
    packed = _pack_all(inputs)
    cur = np.ascontiguousarray(np.asarray(inputs["L0_attn_x"], np.float32).reshape(8192, 2048))
    for seg in SEGMENTS:
        nc, names = build_nc(seg)
        in_maps = []
        for c in range(8):
            m = {"x": np.ascontiguousarray(cur[c * 1024:(c + 1) * 1024])}
            for n in names:
                m[n] = packed[n]
            in_maps.append(m)
        res = run_bass_kernel_spmd(nc, in_maps, core_ids=list(range(8)))
        cur = np.concatenate([r["y"] for r in res.results], axis=0)
    return cur.reshape(1, 8192, 2048).astype(np.float32)
